# version 23
# baseline (speedup 1.0000x reference)
import numpy as np
import ml_dtypes
from contextlib import ExitStack

import concourse.bass as bass
import concourse.mybir as mybir
from concourse.bass_utils import run_bass_kernel_spmd

F32 = mybir.dt.float32
BF16 = mybir.dt.bfloat16
AF = mybir.ActivationFunctionType
ALU = mybir.AluOpType
AX = mybir.AxisListType

NC = 8
D = 2048
H = 16
T = 2048
TT = 512
NTILE = T // TT
NSUB = TT // 128
NG = TT // 8
NB = NG // 4
KC = D // 128
M32 = 12582912.0
M52 = 6755399441055744.0
INV_SQRT_HD = 0.08838834764831845
WEIGHTS = ("wq", "wk", "wv", "wf", "wo")


class Sem:
    def __init__(self, stack, nc, name):
        self.h = stack.enter_context(nc.semaphore(name))
        self.n = 0

    def tick(self, k=1):
        self.n += k
        return self.n


class Prog:

    def __init__(self):
        self.streams = {k: [] for k in ("sync", "tensor", "vector", "scalar",
                                        "gpsimd")}
        self.auto = {}

    def emit(self, eng, builder, waits=(), inc=None, inck=1, is_dma=False):
        auto = self.auto.get(eng)
        w = [(s, v) for (s, v) in waits if v is not None]
        incs = []
        if auto is not None and not is_dma:
            assert inc is None or inc is auto, "compute op must inc own sem"
            if auto.n > 0:
                w.append((auto, auto.n))
            incs.append((auto, 1))
            val = auto.tick(1)
        else:
            if inc is not None:
                incs.append((inc, inck))
                val = inc.tick(inck)
            else:
                val = None
        rw = tuple((s.h, v) for (s, v) in w)
        ri = tuple((s.h, k) for (s, k) in incs)

        def fn(e, builder=builder, rw=rw, ri=ri):
            for h, v in rw:
                e.wait_ge(h, v)
            inst = builder(e)
            for h, k in ri:
                inst = inst.then_inc(h, k)
        self.streams[eng].append(fn)
        return val

    def wait(self, eng, sem, val):
        self.streams[eng].append(lambda e, h=sem.h, v=val: e.wait_ge(h, v))


def build_nc():
    nc = bass.Bass("TRN2", target_bir_lowering=False)
    x_in = nc.dram_tensor("x", [T, D], F32, kind="ExternalInput").ap()
    w_in = {w: nc.dram_tensor(w, [D, D], F32, kind="ExternalInput").ap()
            for w in WEIGHTS}
    ident_in = nc.dram_tensor("ident", [128, 128], BF16, kind="ExternalInput").ap()
    mask_in = nc.dram_tensor("maskc", [128, 512], BF16, kind="ExternalInput").ap()
    out_d = nc.dram_tensor("out", [T, D], F32, kind="ExternalOutput").ap()
    wT = nc.dram_tensor("wT", [5, 4, 128, KC, 512], BF16, kind="Internal").ap()
    xstage = nc.dram_tensor("xstage", [2, 2, TT, D], BF16, kind="Internal").ap()
    wbf = nc.dram_tensor("wbf", [5, D, D], BF16, kind="Internal").ap()

    st = ExitStack()
    st.__enter__()

    def sb(name, shape, dt):
        return st.enter_context(nc.sbuf_tensor(name, shape, dt))

    def pt_(name, shape, dt):
        return st.enter_context(nc.psum_tensor(name, shape, dt))

    x_sub = sb("x_sub", [128, D], F32)
    q2 = sb("q2", [128, D], F32)
    xnTab = [sb(f"xnT{i}", [128, KC * TT], BF16) for i in range(2)]
    attnT = sb("attnT", [128, KC * TT], BF16)
    xqT = sb("xqT", [128, KC * TT], BF16)
    QT = sb("QT", [128, NG * 128], BF16)
    KT = sb("KT", [128, NG * 128], BF16)
    VT = sb("VT", [128, NG * 128], BF16)
    wcb = [sb(f"wcb{i}", [128, KC * 512], BF16) for i in range(3)]
    Abuf = [sb(f"Abuf{i}", [128, 512], BF16) for i in range(3)]
    AVs = [sb(f"AVs{i}", [128, 1024], BF16) for i in range(2)]
    idents = sb("idents", [128, 128], BF16)
    masks = sb("masks", [128, 512], BF16)
    sums4 = sb("sums4", [128, 4], F32)
    rs4 = sb("rs4", [128, 4], F32)
    small = sb("small", [128, 8], F32)
    ascale_t = sb("ascale_t", [128, 2 * NSUB], F32)
    ascm = sb("ascm", [128, 2 * NSUB], F32)
    sS = sb("sS", [128, 1], F32)
    s1mS = sb("s1mS", [128, 1], F32)
    sA = sb("sA", [128, 2], F32)
    onescol = sb("onescol", [128, 1], F32)
    onesrow = sb("onesrow", [1, 128], F32)
    wsacc = sb("wsacc", [128, 1], F32)
    wstmp = sb("wstmp", [128, 1], F32)
    ws_sc = sb("ws_sc", [1, 2], F32)
    ws_bc = sb("ws_bc", [128, 2], F32)
    dsc = sb("dsc", [128, 4], F32)
    wf32 = [sb(f"wf32{i}", [128, 1024], F32) for i in range(2)]
    identf32 = sb("identf32", [128, 128], F32)
    wsacc32 = sb("wsacc32", [128, 32], F32)

    xr = [sb(f"xr{i}", [128, 512], F32) for i in range(4)]

    pj = [pt_(f"pj{i}", [128, 512], F32) for i in range(2)]
    ps = pt_("ps", [128, 512], F32)
    pav = [pt_(f"pav{i}", [128, 512], F32) for i in range(2)]
    ptr = [pt_(f"ptr{i}", [128, 1024], BF16) for i in range(2)]
    pt32 = pt_("pt32", [128, 512], F32)

    P = Prog()
    sCst = Sem(st, nc, "scst")
    sWf32 = [Sem(st, nc, f"swf{i}") for i in range(2)]
    sWc = [Sem(st, nc, f"swc{i}") for i in range(2)]
    sWfq = [Sem(st, nc, f"swfq{i}") for i in range(2)]
    sTw = [Sem(st, nc, f"stw{i}") for i in range(2)]
    sSw = [Sem(st, nc, f"ssw{i}") for i in range(2)]
    sWcb = [Sem(st, nc, f"swcb{i}") for i in range(3)]
    sX = Sem(st, nc, "sx")
    sXr = [Sem(st, nc, f"sxr{i}") for i in range(4)]
    sXo = [Sem(st, nc, f"sxo{i}") for i in range(2)]
    sTi = [Sem(st, nc, f"sti{i}") for i in range(2)]
    sFT = Sem(st, nc, "sft")
    sFO = Sem(st, nc, "sfo")
    sPW = Sem(st, nc, "spw")
    sOut = [Sem(st, nc, f"sout{i}") for i in range(2)]
    PE = Sem(st, nc, "pe")
    VE = Sem(st, nc, "ve")
    SE = Sem(st, nc, "se")
    PO = Sem(st, nc, "po")
    P.auto = {"vector": VE, "scalar": SE, "gpsimd": PO}

    cmag = sb("cmag", [128, 4], F32)
    for col, val in enumerate((M32, -M32, M52, -M52)):
        nc.const_aps.register(F32, float(val), cmag[:, col:col + 1])

    P.emit("sync", lambda e: e.dma_start(out=idents[:], in_=ident_in), inc=sCst, inck=16)
    P.emit("sync", lambda e: e.dma_start(out=masks[:], in_=mask_in), inc=sCst, inck=16)
    for col, val in enumerate((M32, -M32, M52, -M52)):
        P.emit("vector", lambda e, col=col, val=val: e.memset(cmag[:, col:col + 1], val))
    D32 = float(np.float32(np.sqrt(np.float32(128.0))))
    P.emit("vector", lambda e, D32=D32: e.memset(dsc[:, 0:1], D32))
    P.emit("vector", lambda e: e.reciprocal(dsc[:, 1:2], dsc[:, 0:1]))
    P.emit("vector", lambda e: e.memset(dsc[:, 2:3], 127.0))
    ev_dsc = P.emit("vector", lambda e: e.reciprocal(dsc[:, 3:4], dsc[:, 2:3]))
    P.emit("vector", lambda e: e.memset(sS[:], 0.7))
    P.emit("vector", lambda e: e.tensor_scalar(sS[:], sS[:], M52, -M52, ALU.add, ALU.add))
    P.emit("vector", lambda e: e.tensor_scalar(s1mS[:], sS[:], -1.0, 1.0, ALU.mult, ALU.add))
    P.emit("vector", lambda e: e.tensor_scalar(sA[:, 0:1], sS[:], M52, M32, ALU.mult, ALU.add))
    P.emit("vector", lambda e: e.tensor_scalar(sA[:, 1:2], sA[:, 0:1], -1.0, None, ALU.mult))
    P.emit("vector", lambda e: e.memset(onescol[:], 1.0))
    ev_ones = P.emit("vector", lambda e: e.memset(onesrow[:], 1.0), inc=VE)
    ev_cinit = P.emit("vector", lambda e: e.memset(wsacc[:], 0.0), inc=VE)
    ev_idf = P.emit("vector", lambda e: e.tensor_copy(
        identf32[:], idents[:]), waits=[(sCst, 32)], inc=VE)
    P.wait("scalar", VE, ev_cinit)

    w_done = {}
    wf32_free = [None, None]
    wstage_free = [None, None]
    co_prev = [None, None]
    state = {"ev_bc_v": None, "pq_free": None}

    def prologue_cast(wi, wname):
        wsrc = w_in[wname]
        for blkr in range(4):
            cow = []
            if co_prev[blkr % 2] is not None:
                cow.append(co_prev[blkr % 2])
            ev_co = P.emit("gpsimd", lambda e, wi=wi, blkr=blkr: e.dma_start(
                out=wbf[wi][blkr * 512:(blkr + 1) * 512, :],
                in_=wsrc[blkr * 512:(blkr + 1) * 512, :]),
                waits=cow, inc=sWc[blkr % 2], inck=16, is_dma=True)
            co_prev[blkr % 2] = (sWc[blkr % 2], ev_co)
        return [(sWc[0], sWc[0].n), (sWc[1], sWc[1].n)]

    def wf_pass1():
        lds = []
        ev_acc = None
        for idx in range(32):
            r2, hh, par = idx // 2, idx % 2, idx % 2
            w = []
            if len(lds) >= 2:
                w.append(lds[-2])
            ev_ld = P.emit("sync", lambda e, r2=r2, hh=hh, par=par: e.dma_start(
                out=wf32[par][:],
                in_=w_in["wf"][r2 * 128:(r2 + 1) * 128,
                               hh * 1024:(hh + 1) * 1024]),
                waits=w, inc=sWf32[par], inck=16)
            aw = [(sWf32[par], ev_ld)]
            if idx == 0 and x_state["q2_out"] is not None:
                aw.append(x_state["q2_out"])
            ev_acc = P.emit("scalar", lambda e, par=par, idx=idx: e.activation(
                q2[:, 0:1024], wf32[par][:], AF.Abs,
                accum_out=wsacc32[:, idx:idx + 1]), waits=aw, inc=SE)
            lds.append((SE, ev_acc))
            wf32_free[par] = (SE, ev_acc)
        x_state["q2_extra"] = (SE, ev_acc)
        state["acc_evt"] = (SE, ev_acc)

    def wf_pass1_fin():
        ev_r = P.emit("vector", lambda e: e.reduce_sum(
            wsacc[:], wsacc32[:], axis=AX.X), waits=[state["acc_evt"]],
            inc=VE)
        acc_evt = (VE, ev_r)
        fw = [acc_evt, (VE, ev_ones)]
        if pw(pav[0]) is not None:
            fw.append(pw(pav[0]))
        ev_ws_mm = P.emit("tensor", lambda e: e.matmul(
            pav[0][0:1, 0:1], wsacc[:], onescol[:],
            start=True, stop=True), waits=fw, inc=PE)
        P.emit("vector", lambda e: e.tensor_scalar(
            ws_sc[0:1, 0:1], pav[0][0:1, 0:1], 1.0 / (D * D), 1e-6,
            ALU.mult, ALU.max), waits=[(PE, ev_ws_mm)])
        P.emit("vector", lambda e: e.reciprocal(ws_sc[0:1, 1:2], ws_sc[0:1, 0:1]))
        ev_sc2 = P.emit("vector", lambda e: e.tensor_scalar(
            ws_sc[0:1, 0:1], ws_sc[0:1, 0:1], dsc[0:1, 3:4], None, ALU.mult),
            inc=VE)
        ev_bc = P.emit("tensor", lambda e: e.matmul(
            pav[0][:, 0:2], onesrow[:], ws_sc[0:1, :], start=True, stop=True),
            waits=[(VE, ev_sc2)], inc=PE)
        ev_bc_v = P.emit("vector", lambda e: e.tensor_copy(
            ws_bc[:], pav[0][:, 0:2]), waits=[(PE, ev_bc)], inc=VE)
        state["ev_bc_v"] = (VE, ev_bc_v)
        pfree["pav0"] = (VE, ev_bc_v)

    def wf_quant_sec(k):
        for idx in range(8 * k, 8 * k + 8):
            r, hf, par = idx // 2, idx % 2, idx % 2
            ev_ld = P.emit("sync", lambda e, r=r, hf=hf, par=par: e.dma_start(
                out=wf32[par][:],
                in_=w_in["wf"][r * 128:(r + 1) * 128, hf * 1024:(hf + 1) * 1024]),
                waits=[wf32_free[par]] if wf32_free[par] else [],
                inc=sWfq[par], inck=16)
            qw = [(sWfq[par], ev_ld), state["ev_bc_v"]]
            if state["pq_free"] is not None:
                qw.append(state["pq_free"])
            if idx == 8 * k and x_state["q2_out"] is not None:
                qw.append(x_state["q2_out"])
            P.emit("vector", lambda e, par=par: e.tensor_scalar(
                q2[:, 0:1024], wf32[par][:], ws_bc[:, 1:2], 0.5, ALU.mult, ALU.is_ge),
                waits=qw)
            ev_q = P.emit("vector", lambda e, par=par: e.tensor_scalar(
                q2[:, 1024:2048], wf32[par][:], ws_bc[:, 1:2], -0.5, ALU.mult, ALU.is_le),
                inc=VE)
            wf32_free[par] = (VE, ev_q)
            ev_cl = P.emit("vector", lambda e: e.tensor_tensor(
                q2[:, 1024:2048], q2[:, 0:1024], q2[:, 1024:2048], ALU.subtract), inc=VE)
            cow = [(VE, ev_cl)]
            if co_prev[par] is not None:
                cow.append(co_prev[par])
            ev_co = P.emit("gpsimd", lambda e, r=r, hf=hf:
                           e.dma_start(out=wbf[3][r * 128:(r + 1) * 128,
                                                  hf * 1024:(hf + 1) * 1024],
                                       in_=q2[:, 1024:2048]),
                           waits=cow, inc=sWc[par], inck=16, is_dma=True)
            co_prev[par] = (sWc[par], ev_co)
            state["pq_free"] = (sWc[par], ev_co)
        x_state["q2_extra"] = state["pq_free"]
        return [(sWc[0], sWc[0].n), (sWc[1], sWc[1].n)]

    def prologue_transp(wi, wname, cast_done):
        for c in range(KC):
            p = c % 2
            stg = xnTab[1][:, p * 2048:(p + 1) * 2048]
            w = list(cast_done) if c == 0 else []
            if wstage_free[p] is not None:
                w.append(wstage_free[p])
            ev_ti = P.emit("sync", lambda e, wi=wi, c=c, stg=stg:
                           e.dma_start_transpose(
                               stg, wbf[wi][:, c * 128:(c + 1) * 128]),
                           waits=w, inc=sTw[p], inck=16)
            dst = wT[wi][:, :, c, :].rearrange("oc i o -> i oc o")
            ev_st = P.emit("sync", lambda e, dst=dst, stg=stg: e.dma_start(
                out=dst, in_=stg.rearrange("i (oc o) -> i oc o", o=512)),
                waits=[(sTw[p], ev_ti)], inc=sSw[p], inck=16)
            wstage_free[p] = (sSw[p], ev_st)
        w_done[wname] = [(sSw[0], sSw[0].n), (sSw[1], sSw[1].n)]
    pfree = {}

    def pw(bank):
        return pfree.get(bank.name)

    def pset(bank, evt):
        pfree[bank.name] = evt

    wcb_free = [[], [], []]
    pw_prev = [None]
    ab_free = [None, None, None]
    avs_free = [None, None]
    xr_free = [None, None, None, None]
    out_prev = [None, None]
    wcb_state = {"i": 0}
    x_state = {"xn_last_s": None, "xn_out": None, "q2_out": None}
    xn_ready = {}
    xq_ready = {}
    slot_free = {}
    slot_parts = {}
    qkv_done = {}
    attn_done = {}
    b_last = {}
    de_ffn_last = {}

    a_ready = {}

    def phase_A_slow_sub(i, s):
        row0 = i * TT + s * 128
        sc = (i % 2) * NSUB + s
        xw = [e for e in (x_state["xn_last_s"], x_state["xn_out"]) if e]
        ev_x = P.emit("scalar", lambda e, row0=row0: e.dma_start(
            out=x_sub[:], in_=x_in[row0:row0 + 128, :]), waits=xw,
            inc=sX, inck=16, is_dma=True)
        sqw = [(sX, ev_x)]
        if x_state["q2_out"] is not None:
            sqw.append(x_state["q2_out"])
        if x_state.get("q2_extra") is not None:
            sqw.append(x_state.pop("q2_extra"))
        P.emit("scalar", lambda e: e.activation(
            q2[:], x_sub[:], AF.Copy, accum_out=small[:, 0:1]), waits=sqw)
        ev_sq = P.emit("scalar", lambda e: e.activation(
            q2[:], x_sub[:], AF.Square, accum_out=small[:, 1:2]), inc=SE)
        P.emit("vector", lambda e: e.tensor_scalar(
            small[:, 0:1], small[:, 0:1], 1.0 / D, None, ALU.mult),
            waits=[(SE, ev_sq)])
        P.emit("vector", lambda e: e.tensor_scalar(
            small[:, 2:3], small[:, 0:1], small[:, 0:1], -1e-5,
            ALU.mult, ALU.add))
        ev_var = P.emit("vector", lambda e: e.tensor_scalar(
            small[:, 1:2], small[:, 1:2], 1.0 / D, small[:, 2:3],
            ALU.mult, ALU.subtract), inc=VE)
        P.emit("scalar", lambda e: e.activation(
            small[:, 3:4], small[:, 1:2], AF.Ln), waits=[(VE, ev_var)])
        ev_rstd = P.emit("scalar", lambda e: e.activation(
            small[:, 3:4], small[:, 3:4], AF.Exp, scale=-0.5), inc=SE)
        ev_ngm = P.emit("vector", lambda e: e.tensor_scalar(
            small[:, 2:3], small[:, 0:1], small[:, 3:4], -1.0,
            ALU.mult, ALU.mult), waits=[(SE, ev_rstd)], inc=VE)
        ev_xn = P.emit("vector", lambda e: e.tensor_scalar(
            x_sub[:], x_sub[:], small[:, 3:4], small[:, 2:3],
            ALU.mult, ALU.add), waits=[(VE, ev_ngm)], inc=VE)
        P.emit("vector", lambda e: e.reduce_max(
            small[:, 4:5], x_sub[:], axis=AX.X, apply_absolute_value=True),
            waits=[(VE, ev_xn)])
        P.emit("vector", lambda e, sc=sc: e.tensor_scalar(
            ascale_t[:, sc:sc + 1], small[:, 4:5], 1e-8, None, ALU.max))
        P.emit("vector", lambda e, sc=sc: e.reciprocal(
            small[:, 5:6], ascale_t[:, sc:sc + 1]))
        ev_ra = P.emit("vector", lambda e: e.tensor_scalar(
            small[:, 5:6], small[:, 5:6], 127.0, None, ALU.mult), inc=VE)
        ev_z = P.emit("scalar", lambda e: e.activation(
            q2[:], x_sub[:], AF.Copy, scale=small[:, 5:6]),
            waits=[(VE, ev_ra)], inc=SE)
        x_state["xn_last_s"] = (SE, ev_z)
        P.emit("vector", lambda e: e.tensor_scalar(
            q2[:], q2[:], sA[:, 0:1], sA[:, 1:2], ALU.add, ALU.add),
            waits=[(SE, ev_z)])
        ev_xq = P.emit("vector", lambda e: e.tensor_scalar(
            q2[:], q2[:], 127.0, -127.0, ALU.min, ALU.max), inc=VE)
        par = i % 2
        ow = [(VE, ev_xn)] + slot_free.get(par, [])
        ev_oxn = P.emit("gpsimd", lambda e, par=par, s=s: e.dma_start(
            out=xstage[par, 0, s * 128:(s + 1) * 128, :], in_=x_sub[:]),
            waits=ow, inc=sXo[0], inck=16, is_dma=True)
        x_state["xn_out"] = (sXo[0], ev_oxn)
        ev_oxq = P.emit("gpsimd", lambda e, par=par, s=s: e.dma_start(
            out=xstage[par, 1, s * 128:(s + 1) * 128, :], in_=q2[:]),
            waits=[(VE, ev_xq)], inc=sXo[1], inck=16, is_dma=True)
        x_state["q2_out"] = (sXo[1], ev_oxq)
        if s == 3:
            xn_ready[i] = (sXo[0], sXo[0].n)
            xq_ready[i] = (sXo[1], sXo[1].n)

    def emit_T_ins(i, which, gate=()):
        par = i % 2
        dstT = xnTab[i % 2] if which == 0 else xqT
        sem = sTi[which]
        for c in range(KC):
            w = [xn_ready[i] if which == 0 else xq_ready[i]] + list(gate)
            P.emit("sync", lambda e, c=c, par=par, dstT=dstT, which=which:
                   e.dma_start_transpose(
                dstT[:, c * TT:(c + 1) * TT],
                xstage[par, which][:, c * 128:(c + 1) * 128]),
                waits=w if c == 0 else [], inc=sem, inck=16, is_dma=True)
        ready = (sem, sem.n)
        if which == 0:
            xn_ready[i] = ready
        else:
            xq_ready[i] = ready
        slot_parts[(par, which)] = ready
        slot_free[par] = [v for v in (slot_parts.get((par, 0)),
                                      slot_parts.get((par, 1))) if v]

    pe_state = {"toks": {}, "tb_i": 0}

    def pe_fill_oc(oc, par):
        tb = [pt32, pav[1]]
        lastv = lasts = None
        first_evac = True
        for cg in range(4):
            lds = []
            for r in range(4):
                w = []
                if xr_free[r] is not None:
                    w.append(xr_free[r])
                ev = P.emit("sync", lambda e, oc=oc, r=r, cg=cg: e.dma_start(
                    out=xr[r][:],
                    in_=w_in["wq"][(oc * 4 + r) * 128:(oc * 4 + r + 1) * 128,
                                   cg * 512:(cg + 1) * 512]),
                    waits=w, inc=sXr[r], inck=16)
                lds.append(ev)
            ev_tr = None
            for c2 in range(4):
                c = cg * 4 + c2
                bank = tb[pe_state["tb_i"] % 2]
                pe_state["tb_i"] += 1
                for r in range(4):
                    tw = []
                    if r == 0:
                        if c2 == 0:
                            tw += [(sXr[rr], lds[rr]) for rr in range(4)]
                        if pw(bank) is not None:
                            tw.append(pw(bank))
                        tw.append((VE, ev_idf))
                    ev_tr = P.emit(
                        "tensor", lambda e, bank=bank, r=r, c2=c2:
                        e.transpose(bank[:, r * 128:(r + 1) * 128],
                                    xr[r][:, c2 * 128:(c2 + 1) * 128],
                                    identf32[:]),
                        waits=tw, inc=(PE if r == 3 else None))
                eng = "vector" if c % 2 == 0 else "scalar"
                sem = VE if eng == "vector" else SE
                ew = [(PE, ev_tr)]
                if first_evac:
                    first_evac = False
                    ew += wcb_free[par]
                if eng == "vector":
                    ev_ev = P.emit(eng, lambda e, par=par, c=c, bank=bank:
                                   e.tensor_copy(
                                       wcb[par][:, c * 512:(c + 1) * 512],
                                       bank[:]), waits=ew, inc=sem)
                    lastv = ev_ev
                else:
                    ev_ev = P.emit(eng, lambda e, par=par, c=c, bank=bank:
                                   e.activation(
                                       wcb[par][:, c * 512:(c + 1) * 512],
                                       bank[:], AF.Copy), waits=ew, inc=sem)
                    lasts = ev_ev
                pset(bank, (sem, ev_ev))
            for r in range(4):
                xr_free[r] = (PE, ev_tr)
        toks = [(VE, lastv), (SE, lasts)]
        stw = list(toks)
        if pw_prev[0] is not None:
            stw.append(pw_prev[0])
        ev_st = P.emit("sync", lambda e, oc=oc, par=par: e.dma_start(
            out=wT[0, oc].rearrange("i c o -> i (c o)"), in_=wcb[par][:]),
            waits=stw, inc=sPW, inck=16)
        pw_prev[0] = (sPW, ev_st)
        pe_state["toks"][oc] = (toks, [(sPW, ev_st)])
        return pe_state["toks"][oc]

    def wq_fill(oc, par):
        if oc in pe_state["toks"]:
            return pe_state["toks"][oc]
        return pe_fill_oc(oc, par)

    def phase_B(i, hooks=(), fills=None):
        evs = []
        xnT = xnTab[i % 2]
        xw = [xn_ready[i]]
        pjB = [pj[0], pj[1], ps, pav[0]]
        for pidx, (wname, dstT) in enumerate((("wq", QT), ("wk", KT),
                                              ("wv", VT))):
            wi = WEIGHTS.index(wname)
            for oc in range(4):
                par = wcb_state["i"] % 3
                if fills and wname in fills:
                    wtok, free_extra = fills[wname](oc, par)
                else:
                    waits = list(w_done[wname]) + wcb_free[par]
                    ev_w = P.emit("sync", lambda e, wi=wi, oc=oc, par=par:
                                  e.dma_start(out=wcb[par][:],
                                              in_=wT[wi, oc].rearrange(
                                                  "i c o -> i (c o)")),
                                  waits=waits, inc=sWcb[par], inck=16)
                    wtok, free_extra = [(sWcb[par], ev_w)], []
                wcb_state["i"] += 1
                ev_mm = None
                for j in range(4):
                    h = oc * 4 + j
                    bank = pjB[(oc * 4 + j) % 4]
                    mw = list(wtok)
                    if pw(bank) is not None:
                        mw.append(pw(bank))
                    if oc == 0 and j == 0:
                        mw = mw + xw
                    for c in range(KC):
                        ev_mm = P.emit(
                            "tensor",
                            lambda e, par=par, j=j, c=c, bank=bank,
                            xnT=xnT: e.matmul(
                                bank[:],
                                wcb[par][:, c * 512 + j * 128:c * 512 + (j + 1) * 128],
                                xnT[:, c * TT:(c + 1) * TT],
                                start=(c == 0), stop=(c == KC - 1)),
                            waits=mw if c == 0 else [],
                            inc=(PE if c == KC - 1 else None))
                    if j == 3:
                        wcb_free[par] = [(PE, ev_mm)] + free_extra
                    dst = dstT[:].rearrange("d (g j) -> d g j", j=128)[
                        :, :, h * 8:h * 8 + 8]
                    src = bank[:].rearrange("d (g u) -> d g u", u=8)
                    eng = "vector" if (oc + j) % 2 == 0 else "scalar"
                    sem = VE if eng == "vector" else SE
                    if eng == "vector":
                        ev_ev = P.emit(eng, lambda e, dst=dst, src=src:
                                       e.tensor_copy(dst, src),
                                       waits=[(PE, ev_mm)], inc=sem)
                    else:
                        ev_ev = P.emit(eng, lambda e, dst=dst, src=src:
                                       e.activation(dst, src, AF.Copy),
                                       waits=[(PE, ev_mm)], inc=sem)
                    pset(bank, (sem, ev_ev))
                    evs.append((sem, ev_ev))
            if pidx < len(hooks):
                hooks[pidx]()
        qkv_done[i] = evs
        b_last[i] = (PE, ev_mm)

    def phase_C(i, hooks=()):
        evs = []
        pending = []

        def process(item):
            (b, apar, ev_nrm) = item
            tbank = ptr[b % 2]
            ev_tr = None
            for j in range(4):
                g = b * 4 + j
                tw = [(VE, ev_nrm[j]), (sCst, 32)]
                if j == 0 and pw(tbank) is not None:
                    tw.append(pw(tbank))
                P.emit("tensor", lambda e, apar=apar, j=j, tbank=tbank:
                       e.transpose(tbank[:, j * 256:j * 256 + 128],
                                   Abuf[apar][:, j * 128:(j + 1) * 128],
                                   idents[:]), waits=tw)
                ev_tr = P.emit("tensor", lambda e, g=g, j=j, tbank=tbank:
                               e.transpose(tbank[:, j * 256 + 128:(j + 1) * 256],
                                           VT[:, g * 128:(g + 1) * 128],
                                           idents[:]),
                               inc=(PE if j == 3 else None))
            ab_free[apar] = (PE, ev_tr)
            vpar = b % 2
            ew = [(PE, ev_tr)]
            if avs_free[vpar] is not None:
                ew.append(avs_free[vpar])
            eng = "vector" if b % 2 == 0 else "scalar"
            sem = VE if eng == "vector" else SE
            if eng == "vector":
                ev_ev = P.emit(eng, lambda e, vpar=vpar, tbank=tbank:
                               e.tensor_copy(AVs[vpar][:], tbank[:]),
                               waits=ew, inc=sem)
            else:
                ev_ev = P.emit(eng, lambda e, vpar=vpar, tbank=tbank:
                               e.activation(AVs[vpar][:], tbank[:], AF.Copy),
                               waits=ew, inc=sem)
            pset(tbank, (sem, ev_ev))
            pbank = pav[b % 2]
            ev_av = None
            for j in range(4):
                avw = [(sem, ev_ev)] if j == 0 else []
                if j == 0 and pw(pbank) is not None:
                    avw.append(pw(pbank))
                ev_av = P.emit("tensor", lambda e, vpar=vpar, j=j, pbank=pbank:
                               e.matmul(pbank[:, j * 128:(j + 1) * 128],
                                        AVs[vpar][:, j * 256 + 128:(j + 1) * 256],
                                        AVs[vpar][:, j * 256:j * 256 + 128],
                                        start=True, stop=True),
                               waits=avw, inc=(PE if j == 3 else None))
            avs_free[vpar] = (PE, ev_av)
            dst = attnT[:].rearrange("d (h g u) -> d g h u", h=H, u=8)[
                :, b * 4:(b + 1) * 4, :, :]
            src = pbank[:].rearrange("d (g h u) -> d g h u", h=H, u=8)
            ev_at = P.emit("scalar", lambda e, dst=dst, src=src: e.activation(
                dst, src, AF.Copy), waits=[(PE, ev_av)], inc=SE)
            pset(pbank, (SE, ev_at))
            evs.append((SE, ev_at))

        for b in range(NB):
            if len(pending) > 2:
                process(pending.pop(0))
            apar = b % 3
            sbank = ps if b % 2 == 0 else pj[0]
            sw = [] if b > 0 else list(qkv_done[i])
            if pw(sbank) is not None:
                sw.append(pw(sbank))
            ev_sc = None
            for j in range(4):
                g = b * 4 + j
                ev_sc = P.emit("tensor", lambda e, g=g, j=j, sbank=sbank:
                               e.matmul(sbank[:, j * 128:(j + 1) * 128],
                                        QT[:, g * 128:(g + 1) * 128],
                                        KT[:, g * 128:(g + 1) * 128],
                                        start=True, stop=True),
                               waits=sw if j == 0 else [],
                               inc=(PE if j == 3 else None))
            aw = [(PE, ev_sc), (VE, ev_dsc)]
            if ab_free[apar] is not None:
                aw.append(ab_free[apar])
            ev_exp = P.emit("scalar", lambda e, apar=apar, sbank=sbank:
                            e.activation(Abuf[apar][:], sbank[:], AF.Exp,
                                         scale=dsc[:, 1:2]), waits=aw, inc=SE)
            pset(sbank, (SE, ev_exp))
            ev_msk = P.emit("gpsimd", lambda e, apar=apar: e.tensor_tensor(
                Abuf[apar][:], Abuf[apar][:], masks[:], ALU.mult),
                waits=[(SE, ev_exp), (sCst, 32)], inc=PO)
            P.emit("vector", lambda e, apar=apar: e.reduce_sum(
                sums4[:], Abuf[apar][:].rearrange("p (j n) -> p j n", n=128),
                axis=AX.X), waits=[(PO, ev_msk)])
            ev_rs = P.emit("vector", lambda e: e.reciprocal(rs4[:], sums4[:]),
                           inc=VE)
            ev_nrm = []
            for j in range(4):
                evn = P.emit("vector", lambda e, apar=apar, j=j: e.tensor_scalar(
                    Abuf[apar][:, j * 128:(j + 1) * 128],
                    Abuf[apar][:, j * 128:(j + 1) * 128],
                    rs4[:, j:j + 1], None, ALU.mult),
                    waits=[(VE, ev_rs)] if j == 0 else [], inc=VE)
                ev_nrm.append(evn)
            pending.append((b, apar, ev_nrm))
            if b in (1, 5, 9):
                hi2 = (b - 1) // 4
                if hi2 < len(hooks):
                    hooks[hi2]()
        while pending:
            process(pending.pop(0))
        attn_done[i] = evs

    def phase_DE(i):
        wfid, woid = WEIGHTS.index("wf"), WEIGHTS.index("wo")
        ev_ascm = []
        base = (i % 2) * NSUB
        for m in range(NSUB):
            eva = P.emit("vector", lambda e, m=m, base=base: e.tensor_scalar(
                ascm[:, base + m:base + m + 1],
                ascale_t[:, base + m:base + m + 1], ws_bc[:, 0:1], None,
                ALU.mult), inc=VE)
            ev_ascm.append(eva)
        fbanks = [pj[0], ps]
        obanks = [pj[1], pav[0]]
        for n in range(4):
            parf = wcb_state["i"] % 3
            waits = list(w_done["wf"]) + wcb_free[parf]
            if sFT.n > 0:
                waits.append((sFT, sFT.n))
            for c in range(KC):
                P.emit("scalar", lambda e, n=n, parf=parf, c=c:
                       e.dma_start_transpose(
                           wcb[parf][:, c * 512:(c + 1) * 512],
                           wbf[3][n * 512:(n + 1) * 512,
                                  c * 128:(c + 1) * 128]),
                       waits=waits if c == 0 else [], inc=sFT, inck=16,
                       is_dma=True)
            ev_wf = sFT.n
            wcb_state["i"] += 1
            paro = wcb_state["i"] % 3
            waits = list(w_done["wo"]) + wcb_free[paro]
            if sFO.n > 0:
                waits.append((sFO, sFO.n))
            for c in range(KC):
                P.emit("scalar", lambda e, n=n, paro=paro, c=c:
                       e.dma_start_transpose(
                           wcb[paro][:, c * 512:(c + 1) * 512],
                           wbf[4][n * 512:(n + 1) * 512,
                                  c * 128:(c + 1) * 128]),
                       waits=waits if c == 0 else [], inc=sFO, inck=16,
                       is_dma=True)
            ev_wo = sFO.n
            wcb_state["i"] += 1
            ev_f = ev_o = None
            ev_xrs = []
            for m in range(NSUB):
                row0 = i * TT + m * 128
                xpar = (n * NSUB + m) % 4
                ev_xrs.append(P.emit(
                    "scalar", lambda e, row0=row0, n=n, xpar=xpar:
                    e.dma_start(out=xr[xpar][:],
                                in_=x_in[row0:row0 + 128,
                                         n * 512:(n + 1) * 512]),
                    waits=[xr_free[xpar]] if xr_free[xpar] else [],
                    inc=sXr[xpar], inck=16, is_dma=True))
            for m in range(NSUB):
                row0 = i * TT + m * 128
                xpar = (n * NSUB + m) % 4
                ev_xr = ev_xrs[m]
                fbank = fbanks[m % 2]
                obank = obanks[m % 2]
                fw = [(sFT, ev_wf)]
                if n == 0 and m == 0:
                    fw.append(xq_ready[i])
                if pw(fbank) is not None:
                    fw.append(pw(fbank))
                for c in range(KC):
                    ev_f = P.emit("tensor", lambda e, parf=parf, m=m, c=c,
                                  fbank=fbank:
                                  e.matmul(fbank[:],
                                           xqT[:, c * TT + m * 128:
                                               c * TT + (m + 1) * 128],
                                           wcb[parf][:, c * 512:(c + 1) * 512],
                                           start=(c == 0), stop=(c == KC - 1)),
                                  waits=fw if c == 0 else [],
                                  inc=(PE if c == KC - 1 else None))
                ow = [(sFO, ev_wo)]
                if pw(obank) is not None:
                    ow.append(pw(obank))
                if n == 0 and m == 0:
                    ow = ow + attn_done[i]
                for c in range(KC):
                    ev_o = P.emit("tensor", lambda e, paro=paro, m=m, c=c,
                                  obank=obank:
                                  e.matmul(obank[:],
                                           attnT[:, c * TT + m * 128:
                                                 c * TT + (m + 1) * 128],
                                           wcb[paro][:, c * 512:(c + 1) * 512],
                                           start=(c == 0), stop=(c == KC - 1)),
                                  waits=ow if c == 0 else [],
                                  inc=(PE if c == KC - 1 else None))
                if m == NSUB - 1:
                    wcb_free[parf] = [(PE, ev_f)]
                    wcb_free[paro] = [(PE, ev_o)]
                    if n == 3:
                        de_ffn_last[i] = (PE, ev_f)
                cw = [(PE, ev_f), (sXr[xpar], ev_xr), (VE, ev_ascm[m])]
                ev_cb = P.emit("vector", lambda e, xpar=xpar, m=m,
                               fbank=fbank, base=base:
                               e.scalar_tensor_tensor(
                                   xr[xpar][:], fbank[:],
                                   ascm[:, base + m:base + m + 1],
                                   xr[xpar][:], ALU.mult, ALU.add),
                               waits=cw, inc=VE)
                pset(fbank, (VE, ev_cb))
                ev_fin = P.emit("vector", lambda e, xpar=xpar, obank=obank:
                                e.tensor_tensor(
                    xr[xpar][:], obank[:], xr[xpar][:], ALU.add),
                    waits=[(PE, ev_o), (VE, ev_cb)], inc=VE)
                pset(obank, (VE, ev_fin))
                stw = [(VE, ev_fin)]
                if out_prev[xpar % 2] is not None:
                    stw.append(out_prev[xpar % 2])
                ev_st = P.emit("sync", lambda e, row0=row0, n=n, xpar=xpar:
                               e.dma_start(out=out_d[row0:row0 + 128,
                                                     n * 512:(n + 1) * 512],
                                           in_=xr[xpar][:]),
                               waits=stw, inc=sOut[xpar % 2], inck=16)
                out_prev[xpar % 2] = (sOut[xpar % 2], ev_st)
                xr_free[xpar] = (sOut[xpar % 2], ev_st)

    for sub in range(NSUB):
        phase_A_slow_sub(0, sub)
    cast_done = {}
    for wi, wname in ((1, "wk"), (2, "wv"), (4, "wo")):
        cast_done[wname] = prologue_cast(wi, wname)
    w_done["wo"] = cast_done["wo"]
    emit_T_ins(0, 0)
    emit_T_ins(0, 1)
    wf_pass1()
    for _oc in range(3):
        pe_fill_oc(_oc, _oc)
    for i in range(NTILE):
        nxt = i + 1

        def mk_hook_b(k, i=i, nxt=nxt):
            def h():
                if i == 0:
                    if k == 0:
                        wf_pass1_fin()
                        prologue_transp(1, "wk", cast_done["wk"])
                    if k == 1:
                        prologue_transp(2, "wv", cast_done["wv"])
                    wf_quant_sec(k)
                if nxt < NTILE:
                    phase_A_slow_sub(nxt, k)
            return h

        phase_B(i, [mk_hook_b(0), mk_hook_b(1), mk_hook_b(2)],
                fills={"wq": wq_fill} if i == 0 else None)
        if i == 0:
            wfc = wf_quant_sec(3)
            w_done["wf"] = wfc
            w_done["wq"] = [(sPW, sPW.n)]
        if nxt < NTILE:
            phase_A_slow_sub(nxt, 3)
            if nxt >= 2:
                g = [b_last[nxt - 2]]
            else:
                g = [(sSw[0], sSw[0].n), (sSw[1], sSw[1].n)]
            emit_T_ins(nxt, 0, g)
        phase_C(i)
        phase_DE(i)
        if nxt < NTILE:
            emit_T_ins(nxt, 1, [de_ffn_last[i]])
    for sem in (sCst, sX, sWf32[0], sWf32[1], sWfq[0], sWfq[1],
                sWc[0], sWc[1], sTw[0], sTw[1],
                sSw[0], sSw[1], sWcb[0], sWcb[1], sWcb[2],
                sXr[0], sXr[1], sXr[2], sXr[3], sFT, sFO, sPW,
                sOut[0], sOut[1], sXo[0], sXo[1], sTi[0], sTi[1]):
        P.wait("gpsimd", sem, sem.n)
    P.wait("sync", sOut[0], sOut[0].n)
    P.wait("sync", sOut[1], sOut[1].n)

    with nc.Block() as blk:
        blk.sync(lambda e: [f(e) for f in P.streams["sync"]])
        blk.tensor(lambda e: [f(e) for f in P.streams["tensor"]])
        blk.vector(lambda e: [f(e) for f in P.streams["vector"]])
        blk.scalar(lambda e: [f(e) for f in P.streams["scalar"]])
        blk.gpsimd(lambda e: [f(e) for f in P.streams["gpsimd"]])
    st.__exit__(None, None, None)
    return nc


_NC_CACHE = None


def kernel(L0_attn_x, L0_attn_wq, L0_attn_wk, L0_attn_wv, L0_attn_wo,
           L0_attn_inv_freq, L0_ffn_weight):
    global _NC_CACHE
    assert np.asarray(L0_attn_inv_freq).shape == (64,)

    def _np(a):
        return np.ascontiguousarray(np.asarray(a).astype(np.float32, copy=False))

    x = _np(L0_attn_x).reshape(16384, D)
    ws = {"wq": _np(L0_attn_wq), "wk": _np(L0_attn_wk), "wv": _np(L0_attn_wv),
          "wo": _np(L0_attn_wo), "wf": _np(L0_ffn_weight)}
    ident = np.eye(128, dtype=np.float32).astype(ml_dtypes.bfloat16)
    m1 = (np.arange(128)[:, None] % 8 == np.arange(128)[None, :] % 8)
    maskc = np.tile(m1, (1, 4)).astype(ml_dtypes.bfloat16)

    if _NC_CACHE is None:
        _NC_CACHE = build_nc()
    nc = _NC_CACHE

    in_maps = []
    for c in range(NC):
        im = {"x": x[c * T:(c + 1) * T], "ident": ident, "maskc": maskc}
        im.update(ws)
        in_maps.append(im)
    res = run_bass_kernel_spmd(nc, in_maps, core_ids=list(range(NC)))
    out = np.concatenate([np.asarray(r["out"]) for r in res.results], axis=0)
    return out.reshape(2, 8192, 2048)


# version 24
# speedup vs baseline: 1.1151x; 1.1151x over previous
import numpy as np
import ml_dtypes
from contextlib import ExitStack

import concourse.bass as bass
import concourse.mybir as mybir
from concourse.bass_utils import run_bass_kernel_spmd

F32 = mybir.dt.float32
BF16 = mybir.dt.bfloat16
AF = mybir.ActivationFunctionType
ALU = mybir.AluOpType
AX = mybir.AxisListType

NC = 8
D = 2048
H = 16
T = 2048
TT = 512
NTILE = T // TT
NSUB = TT // 128
NG = TT // 8
NB = NG // 4
KC = D // 128
M32 = 12582912.0
M52 = 6755399441055744.0
INV_SQRT_HD = 0.08838834764831845
WEIGHTS = ("wq", "wk", "wv", "wf", "wo")


class Sem:
    def __init__(self, stack, nc, name):
        self.h = stack.enter_context(nc.semaphore(name))
        self.n = 0

    def tick(self, k=1):
        self.n += k
        return self.n


class Prog:

    def __init__(self):
        self.streams = {k: [] for k in ("sync", "tensor", "vector", "scalar",
                                        "gpsimd")}
        self.auto = {}

    def emit(self, eng, builder, waits=(), inc=None, inck=1, is_dma=False):
        auto = self.auto.get(eng)
        w = [(s, v) for (s, v) in waits if v is not None]
        incs = []
        if auto is not None and not is_dma:
            assert inc is None or inc is auto, "compute op must inc own sem"
            if auto.n > 0:
                w.append((auto, auto.n))
            incs.append((auto, 1))
            val = auto.tick(1)
        else:
            if inc is not None:
                incs.append((inc, inck))
                val = inc.tick(inck)
            else:
                val = None
        rw = tuple((s.h, v) for (s, v) in w)
        ri = tuple((s.h, k) for (s, k) in incs)

        def fn(e, builder=builder, rw=rw, ri=ri):
            for h, v in rw:
                e.wait_ge(h, v)
            inst = builder(e)
            for h, k in ri:
                inst = inst.then_inc(h, k)
        self.streams[eng].append(fn)
        return val

    def wait(self, eng, sem, val):
        self.streams[eng].append(lambda e, h=sem.h, v=val: e.wait_ge(h, v))


def build_nc():
    nc = bass.Bass("TRN2", target_bir_lowering=False)
    x_in = nc.dram_tensor("x", [T, D], F32, kind="ExternalInput").ap()
    w_in = {w: nc.dram_tensor(w, [D, D], F32, kind="ExternalInput").ap()
            for w in WEIGHTS}
    ident_in = nc.dram_tensor("ident", [128, 128], BF16, kind="ExternalInput").ap()
    mask_in = nc.dram_tensor("maskc", [128, 512], BF16, kind="ExternalInput").ap()
    out_d = nc.dram_tensor("out", [T, D], F32, kind="ExternalOutput").ap()
    wT = nc.dram_tensor("wT", [5, 4, 128, KC, 512], BF16, kind="Internal").ap()
    xstage = nc.dram_tensor("xstage", [2, 2, TT, D], BF16, kind="Internal").ap()
    wbf = nc.dram_tensor("wbf", [5, D, D], BF16, kind="Internal").ap()

    st = ExitStack()
    st.__enter__()

    def sb(name, shape, dt):
        return st.enter_context(nc.sbuf_tensor(name, shape, dt))

    def pt_(name, shape, dt):
        return st.enter_context(nc.psum_tensor(name, shape, dt))

    x_sub = sb("x_sub", [128, D], F32)
    q2 = sb("q2", [128, D], F32)
    xnTab = [sb(f"xnT{i}", [128, KC * TT], BF16) for i in range(2)]
    attnT = sb("attnT", [128, KC * TT], BF16)
    xqT = sb("xqT", [128, KC * TT], BF16)
    QT = sb("QT", [128, NG * 128], BF16)
    KT = sb("KT", [128, NG * 128], BF16)
    VT = sb("VT", [128, NG * 128], BF16)
    wcb = [sb(f"wcb{i}", [128, KC * 512], BF16) for i in range(3)]
    Abuf = [sb(f"Abuf{i}", [128, 512], BF16) for i in range(3)]
    AVs = [sb(f"AVs{i}", [128, 1024], BF16) for i in range(2)]
    idents = sb("idents", [128, 128], BF16)
    masks = sb("masks", [128, 512], BF16)
    sums4 = sb("sums4", [128, 4], F32)
    rs4 = sb("rs4", [128, 4], F32)
    small = sb("small", [128, 8], F32)
    ascale_t = sb("ascale_t", [128, 2 * NSUB], F32)
    ascm = sb("ascm", [128, 2 * NSUB], F32)
    sS = sb("sS", [128, 1], F32)
    s1mS = sb("s1mS", [128, 1], F32)
    sA = sb("sA", [128, 2], F32)
    onescol = sb("onescol", [128, 1], F32)
    onesrow = sb("onesrow", [1, 128], F32)
    wsacc = sb("wsacc", [128, 1], F32)
    wstmp = sb("wstmp", [128, 1], F32)
    ws_sc = sb("ws_sc", [1, 2], F32)
    ws_bc = sb("ws_bc", [128, 2], F32)
    dsc = sb("dsc", [128, 4], F32)
    wf32 = [sb(f"wf32{i}", [128, 1024], F32) for i in range(2)]
    identf32 = sb("identf32", [128, 128], F32)
    wsacc32 = sb("wsacc32", [128, 32], F32)

    xr = [sb(f"xr{i}", [128, 512], F32) for i in range(4)]

    pj = [pt_(f"pj{i}", [128, 512], F32) for i in range(2)]
    ps = pt_("ps", [128, 512], F32)
    pav = [pt_(f"pav{i}", [128, 512], F32) for i in range(2)]
    ptr = [pt_(f"ptr{i}", [128, 1024], BF16) for i in range(2)]
    pt32 = pt_("pt32", [128, 512], F32)

    P = Prog()
    sCst = Sem(st, nc, "scst")
    sWf32 = [Sem(st, nc, f"swf{i}") for i in range(2)]
    sWc = [Sem(st, nc, f"swc{i}") for i in range(2)]
    sWfq = [Sem(st, nc, f"swfq{i}") for i in range(2)]
    sTw = [Sem(st, nc, f"stw{i}") for i in range(2)]
    sSw = [Sem(st, nc, f"ssw{i}") for i in range(2)]
    sWcb = [Sem(st, nc, f"swcb{i}") for i in range(3)]
    sX = Sem(st, nc, "sx")
    sXr = [Sem(st, nc, f"sxr{i}") for i in range(4)]
    sXo = [Sem(st, nc, f"sxo{i}") for i in range(2)]
    sTi = [Sem(st, nc, f"sti{i}") for i in range(2)]
    sFT = Sem(st, nc, "sft")
    sFO = Sem(st, nc, "sfo")
    sPW = Sem(st, nc, "spw")
    sOut = [Sem(st, nc, f"sout{i}") for i in range(2)]
    PE = Sem(st, nc, "pe")
    VE = Sem(st, nc, "ve")
    SE = Sem(st, nc, "se")
    PO = Sem(st, nc, "po")
    P.auto = {"vector": VE, "scalar": SE, "gpsimd": PO}

    cmag = sb("cmag", [128, 4], F32)
    for col, val in enumerate((M32, -M32, M52, -M52)):
        nc.const_aps.register(F32, float(val), cmag[:, col:col + 1])

    P.emit("sync", lambda e: e.dma_start(out=idents[:], in_=ident_in), inc=sCst, inck=16)
    P.emit("sync", lambda e: e.dma_start(out=masks[:], in_=mask_in), inc=sCst, inck=16)
    for col, val in enumerate((M32, -M32, M52, -M52)):
        P.emit("vector", lambda e, col=col, val=val: e.memset(cmag[:, col:col + 1], val))
    D32 = float(np.float32(np.sqrt(np.float32(128.0))))
    P.emit("vector", lambda e, D32=D32: e.memset(dsc[:, 0:1], D32))
    P.emit("vector", lambda e: e.reciprocal(dsc[:, 1:2], dsc[:, 0:1]))
    P.emit("vector", lambda e: e.memset(dsc[:, 2:3], 127.0))
    ev_dsc = P.emit("vector", lambda e: e.reciprocal(dsc[:, 3:4], dsc[:, 2:3]))
    P.emit("vector", lambda e: e.memset(sS[:], 0.7))
    P.emit("vector", lambda e: e.tensor_scalar(sS[:], sS[:], M52, -M52, ALU.add, ALU.add))
    P.emit("vector", lambda e: e.tensor_scalar(s1mS[:], sS[:], -1.0, 1.0, ALU.mult, ALU.add))
    P.emit("vector", lambda e: e.tensor_scalar(sA[:, 0:1], sS[:], M52, M32, ALU.mult, ALU.add))
    P.emit("vector", lambda e: e.tensor_scalar(sA[:, 1:2], sA[:, 0:1], -1.0, None, ALU.mult))
    P.emit("vector", lambda e: e.memset(onescol[:], 1.0))
    ev_ones = P.emit("vector", lambda e: e.memset(onesrow[:], 1.0), inc=VE)
    ev_cinit = P.emit("vector", lambda e: e.memset(wsacc[:], 0.0), inc=VE)
    ev_idf = P.emit("vector", lambda e: e.tensor_copy(
        identf32[:], idents[:]), waits=[(sCst, 32)], inc=VE)
    P.wait("scalar", VE, ev_cinit)

    w_done = {}
    wf32_free = [None, None]
    wstage_free = [None, None]
    co_prev = [None, None]
    state = {"ev_bc_v": None, "pq_free": None}

    def prologue_cast(wi, wname):
        wsrc = w_in[wname]
        for blkr in range(4):
            cow = []
            if co_prev[blkr % 2] is not None:
                cow.append(co_prev[blkr % 2])
            ev_co = P.emit("gpsimd", lambda e, wi=wi, blkr=blkr: e.dma_start(
                out=wbf[wi][blkr * 512:(blkr + 1) * 512, :],
                in_=wsrc[blkr * 512:(blkr + 1) * 512, :]),
                waits=cow, inc=sWc[blkr % 2], inck=16, is_dma=True)
            co_prev[blkr % 2] = (sWc[blkr % 2], ev_co)
        return [(sWc[0], sWc[0].n), (sWc[1], sWc[1].n)]

    def wf_pass1():
        lds = []
        ev_acc = None
        for idx in range(32):
            r2, hh, par = idx // 2, idx % 2, idx % 2
            w = []
            if len(lds) >= 2:
                w.append(lds[-2])
            ev_ld = P.emit("sync", lambda e, r2=r2, hh=hh, par=par: e.dma_start(
                out=wf32[par][:],
                in_=w_in["wf"][r2 * 128:(r2 + 1) * 128,
                               hh * 1024:(hh + 1) * 1024]),
                waits=w, inc=sWf32[par], inck=16)
            aw = [(sWf32[par], ev_ld)]
            if idx == 0 and x_state["q2_out"] is not None:
                aw.append(x_state["q2_out"])
            ev_acc = P.emit("scalar", lambda e, par=par, idx=idx: e.activation(
                q2[:, 0:1024], wf32[par][:], AF.Abs,
                accum_out=wsacc32[:, idx:idx + 1]), waits=aw, inc=SE)
            lds.append((SE, ev_acc))
            wf32_free[par] = (SE, ev_acc)
        x_state["q2_extra"] = (SE, ev_acc)
        state["acc_evt"] = (SE, ev_acc)

    def wf_pass1_fin():
        ev_r = P.emit("vector", lambda e: e.reduce_sum(
            wsacc[:], wsacc32[:], axis=AX.X), waits=[state["acc_evt"]],
            inc=VE)
        acc_evt = (VE, ev_r)
        fw = [acc_evt, (VE, ev_ones)]
        if pw(pav[0]) is not None:
            fw.append(pw(pav[0]))
        ev_ws_mm = P.emit("tensor", lambda e: e.matmul(
            pav[0][0:1, 0:1], wsacc[:], onescol[:],
            start=True, stop=True), waits=fw, inc=PE)
        P.emit("vector", lambda e: e.tensor_scalar(
            ws_sc[0:1, 0:1], pav[0][0:1, 0:1], 1.0 / (D * D), 1e-6,
            ALU.mult, ALU.max), waits=[(PE, ev_ws_mm)])
        P.emit("vector", lambda e: e.reciprocal(ws_sc[0:1, 1:2], ws_sc[0:1, 0:1]))
        ev_sc2 = P.emit("vector", lambda e: e.tensor_scalar(
            ws_sc[0:1, 0:1], ws_sc[0:1, 0:1], dsc[0:1, 3:4], None, ALU.mult),
            inc=VE)
        ev_bc = P.emit("tensor", lambda e: e.matmul(
            pav[0][:, 0:2], onesrow[:], ws_sc[0:1, :], start=True, stop=True),
            waits=[(VE, ev_sc2)], inc=PE)
        ev_bc_v = P.emit("vector", lambda e: e.tensor_copy(
            ws_bc[:], pav[0][:, 0:2]), waits=[(PE, ev_bc)], inc=VE)
        state["ev_bc_v"] = (VE, ev_bc_v)
        pfree["pav0"] = (VE, ev_bc_v)

    def wf_quant_sec(k):
        for idx in range(8 * k, 8 * k + 8):
            r, hf, par = idx // 2, idx % 2, idx % 2
            ev_ld = P.emit("sync", lambda e, r=r, hf=hf, par=par: e.dma_start(
                out=wf32[par][:],
                in_=w_in["wf"][r * 128:(r + 1) * 128, hf * 1024:(hf + 1) * 1024]),
                waits=[wf32_free[par]] if wf32_free[par] else [],
                inc=sWfq[par], inck=16)
            qw = [(sWfq[par], ev_ld), state["ev_bc_v"]]
            if state["pq_free"] is not None:
                qw.append(state["pq_free"])
            if idx == 8 * k and x_state["q2_out"] is not None:
                qw.append(x_state["q2_out"])
            P.emit("vector", lambda e, par=par: e.tensor_scalar(
                q2[:, 0:1024], wf32[par][:], ws_bc[:, 1:2], 0.5, ALU.mult, ALU.is_ge),
                waits=qw)
            ev_q = P.emit("vector", lambda e, par=par: e.tensor_scalar(
                q2[:, 1024:2048], wf32[par][:], ws_bc[:, 1:2], -0.5, ALU.mult, ALU.is_le),
                inc=VE)
            wf32_free[par] = (VE, ev_q)
            ev_cl = P.emit("vector", lambda e: e.tensor_tensor(
                q2[:, 1024:2048], q2[:, 0:1024], q2[:, 1024:2048], ALU.subtract), inc=VE)
            cow = [(VE, ev_cl)]
            if co_prev[par] is not None:
                cow.append(co_prev[par])
            ev_co = P.emit("gpsimd", lambda e, r=r, hf=hf:
                           e.dma_start(out=wbf[3][r * 128:(r + 1) * 128,
                                                  hf * 1024:(hf + 1) * 1024],
                                       in_=q2[:, 1024:2048]),
                           waits=cow, inc=sWc[par], inck=16, is_dma=True)
            co_prev[par] = (sWc[par], ev_co)
            state["pq_free"] = (sWc[par], ev_co)
        x_state["q2_extra"] = state["pq_free"]
        return [(sWc[0], sWc[0].n), (sWc[1], sWc[1].n)]

    def prologue_transp(wi, wname, cast_done):
        for c in range(KC):
            p = c % 2
            stg = xnTab[1][:, p * 2048:(p + 1) * 2048]
            w = list(cast_done) if c == 0 else []
            if wstage_free[p] is not None:
                w.append(wstage_free[p])
            ev_ti = P.emit("sync", lambda e, wi=wi, c=c, stg=stg:
                           e.dma_start_transpose(
                               stg, wbf[wi][:, c * 128:(c + 1) * 128]),
                           waits=w, inc=sTw[p], inck=16)
            dst = wT[wi][:, :, c, :].rearrange("oc i o -> i oc o")
            ev_st = P.emit("sync", lambda e, dst=dst, stg=stg: e.dma_start(
                out=dst, in_=stg.rearrange("i (oc o) -> i oc o", o=512)),
                waits=[(sTw[p], ev_ti)], inc=sSw[p], inck=16)
            wstage_free[p] = (sSw[p], ev_st)
        w_done[wname] = [(sSw[0], sSw[0].n), (sSw[1], sSw[1].n)]
    pfree = {}

    def pw(bank):
        return pfree.get(bank.name)

    def pset(bank, evt):
        pfree[bank.name] = evt

    wcb_free = [[], [], []]
    pw_prev = [None]
    ab_free = [None, None, None]
    avs_free = [None, None]
    xr_free = [None, None, None, None]
    out_prev = [None, None]
    wcb_state = {"i": 0}
    x_state = {"xn_last_s": None, "xn_out": None, "q2_out": None}
    xn_ready = {}
    xq_ready = {}
    slot_free = {}
    slot_parts = {}
    qkv_done = {}
    attn_done = {}
    b_last = {}
    de_ffn_last = {}

    a_ready = {}

    def phase_A_slow_sub(i, s):
        row0 = i * TT + s * 128
        sc = (i % 2) * NSUB + s
        xw = [e for e in (x_state["xn_last_s"], x_state["xn_out"]) if e]
        ev_x = P.emit("scalar", lambda e, row0=row0: e.dma_start(
            out=x_sub[:], in_=x_in[row0:row0 + 128, :]), waits=xw,
            inc=sX, inck=16, is_dma=True)
        sqw = [(sX, ev_x)]
        if x_state["q2_out"] is not None:
            sqw.append(x_state["q2_out"])
        if x_state.get("q2_extra") is not None:
            sqw.append(x_state.pop("q2_extra"))
        P.emit("scalar", lambda e: e.activation(
            q2[:], x_sub[:], AF.Copy, accum_out=small[:, 0:1]), waits=sqw)
        ev_sq = P.emit("scalar", lambda e: e.activation(
            q2[:], x_sub[:], AF.Square, accum_out=small[:, 1:2]), inc=SE)
        P.emit("vector", lambda e: e.tensor_scalar(
            small[:, 0:1], small[:, 0:1], 1.0 / D, None, ALU.mult),
            waits=[(SE, ev_sq)])
        P.emit("vector", lambda e: e.tensor_scalar(
            small[:, 2:3], small[:, 0:1], small[:, 0:1], -1e-5,
            ALU.mult, ALU.add))
        ev_var = P.emit("vector", lambda e: e.tensor_scalar(
            small[:, 1:2], small[:, 1:2], 1.0 / D, small[:, 2:3],
            ALU.mult, ALU.subtract), inc=VE)
        P.emit("scalar", lambda e: e.activation(
            small[:, 3:4], small[:, 1:2], AF.Ln), waits=[(VE, ev_var)])
        ev_rstd = P.emit("scalar", lambda e: e.activation(
            small[:, 3:4], small[:, 3:4], AF.Exp, scale=-0.5), inc=SE)
        ev_ngm = P.emit("vector", lambda e: e.tensor_scalar(
            small[:, 2:3], small[:, 0:1], small[:, 3:4], -1.0,
            ALU.mult, ALU.mult), waits=[(SE, ev_rstd)], inc=VE)
        ev_xn = P.emit("vector", lambda e: e.tensor_scalar(
            x_sub[:], x_sub[:], small[:, 3:4], small[:, 2:3],
            ALU.mult, ALU.add), waits=[(VE, ev_ngm)], inc=VE)
        P.emit("vector", lambda e: e.reduce_max(
            small[:, 4:5], x_sub[:], axis=AX.X, apply_absolute_value=True),
            waits=[(VE, ev_xn)])
        P.emit("vector", lambda e, sc=sc: e.tensor_scalar(
            ascale_t[:, sc:sc + 1], small[:, 4:5], 1e-8, None, ALU.max))
        P.emit("vector", lambda e, sc=sc: e.reciprocal(
            small[:, 5:6], ascale_t[:, sc:sc + 1]))
        ev_ra = P.emit("vector", lambda e: e.tensor_scalar(
            small[:, 5:6], small[:, 5:6], 127.0, None, ALU.mult), inc=VE)
        ev_z = P.emit("scalar", lambda e: e.activation(
            q2[:], x_sub[:], AF.Copy, scale=small[:, 5:6]),
            waits=[(VE, ev_ra)], inc=SE)
        x_state["xn_last_s"] = (SE, ev_z)
        P.emit("vector", lambda e: e.tensor_scalar(
            q2[:], q2[:], sA[:, 0:1], sA[:, 1:2], ALU.add, ALU.add),
            waits=[(SE, ev_z)])
        ev_xq = P.emit("vector", lambda e: e.tensor_scalar(
            q2[:], q2[:], 127.0, -127.0, ALU.min, ALU.max), inc=VE)
        par = i % 2
        ow = [(VE, ev_xn)] + slot_free.get(par, [])
        ev_oxn = P.emit("gpsimd", lambda e, par=par, s=s: e.dma_start(
            out=xstage[par, 0, s * 128:(s + 1) * 128, :], in_=x_sub[:]),
            waits=ow, inc=sXo[0], inck=16, is_dma=True)
        x_state["xn_out"] = (sXo[0], ev_oxn)
        ev_oxq = P.emit("gpsimd", lambda e, par=par, s=s: e.dma_start(
            out=xstage[par, 1, s * 128:(s + 1) * 128, :], in_=q2[:]),
            waits=[(VE, ev_xq)], inc=sXo[1], inck=16, is_dma=True)
        x_state["q2_out"] = (sXo[1], ev_oxq)
        if s == 3:
            xn_ready[i] = (sXo[0], sXo[0].n)
            xq_ready[i] = (sXo[1], sXo[1].n)

    def emit_T_ins(i, which, gate=()):
        par = i % 2
        dstT = xnTab[i % 2] if which == 0 else xqT
        sem = sTi[which]
        for c in range(KC):
            w = [xn_ready[i] if which == 0 else xq_ready[i]] + list(gate)
            P.emit("scalar", lambda e, c=c, par=par, dstT=dstT, which=which:
                   e.dma_start_transpose(
                dstT[:, c * TT:(c + 1) * TT],
                xstage[par, which][:, c * 128:(c + 1) * 128]),
                waits=w if c == 0 else [], inc=sem, inck=16, is_dma=True)
        ready = (sem, sem.n)
        if which == 0:
            xn_ready[i] = ready
        else:
            xq_ready[i] = ready
        slot_parts[(par, which)] = ready
        slot_free[par] = [v for v in (slot_parts.get((par, 0)),
                                      slot_parts.get((par, 1))) if v]

    pe_state = {"toks": {}, "tb_i": 0}

    def pe_fill_oc(oc, par):
        tb = [pt32, pav[1]]
        lastv = lasts = None
        first_evac = True
        for cg in range(4):
            lds = []
            for r in range(4):
                w = []
                if xr_free[r] is not None:
                    w.append(xr_free[r])
                ev = P.emit("sync", lambda e, oc=oc, r=r, cg=cg: e.dma_start(
                    out=xr[r][:],
                    in_=w_in["wq"][(oc * 4 + r) * 128:(oc * 4 + r + 1) * 128,
                                   cg * 512:(cg + 1) * 512]),
                    waits=w, inc=sXr[r], inck=16)
                lds.append(ev)
            ev_tr = None
            for c2 in range(4):
                c = cg * 4 + c2
                bank = tb[pe_state["tb_i"] % 2]
                pe_state["tb_i"] += 1
                for r in range(4):
                    tw = []
                    if r == 0:
                        if c2 == 0:
                            tw += [(sXr[rr], lds[rr]) for rr in range(4)]
                        if pw(bank) is not None:
                            tw.append(pw(bank))
                        tw.append((VE, ev_idf))
                    ev_tr = P.emit(
                        "tensor", lambda e, bank=bank, r=r, c2=c2:
                        e.transpose(bank[:, r * 128:(r + 1) * 128],
                                    xr[r][:, c2 * 128:(c2 + 1) * 128],
                                    identf32[:]),
                        waits=tw, inc=(PE if r == 3 else None))
                ew = [(PE, ev_tr)]
                if first_evac:
                    first_evac = False
                    ew += wcb_free[par]
                ev_ev = P.emit("scalar", lambda e, par=par, c=c, bank=bank:
                               e.activation(
                                   wcb[par][:, c * 512:(c + 1) * 512],
                                   bank[:], AF.Copy), waits=ew, inc=SE)
                lasts = ev_ev
                pset(bank, (SE, ev_ev))
            for r in range(4):
                xr_free[r] = (PE, ev_tr)
        toks = [(SE, lasts)]
        stw = list(toks)
        if pw_prev[0] is not None:
            stw.append(pw_prev[0])
        ev_st = P.emit("sync", lambda e, oc=oc, par=par: e.dma_start(
            out=wT[0, oc].rearrange("i c o -> i (c o)"), in_=wcb[par][:]),
            waits=stw, inc=sPW, inck=16)
        pw_prev[0] = (sPW, ev_st)
        pe_state["toks"][oc] = (toks, [(sPW, ev_st)])
        return pe_state["toks"][oc]

    def wq_fill(oc, par):
        if oc in pe_state["toks"]:
            return pe_state["toks"][oc]
        return pe_fill_oc(oc, par)

    def phase_B(i, hooks=(), fills=None):
        evs = []
        xnT = xnTab[i % 2]
        xw = [xn_ready[i]]
        pjB = [pj[0], pj[1], ps, pav[0]]
        for pidx, (wname, dstT) in enumerate((("wq", QT), ("wk", KT),
                                              ("wv", VT))):
            wi = WEIGHTS.index(wname)
            for oc in range(4):
                par = wcb_state["i"] % 3
                if fills and wname in fills:
                    wtok, free_extra = fills[wname](oc, par)
                else:
                    waits = list(w_done[wname]) + wcb_free[par]
                    ev_w = P.emit("sync", lambda e, wi=wi, oc=oc, par=par:
                                  e.dma_start(out=wcb[par][:],
                                              in_=wT[wi, oc].rearrange(
                                                  "i c o -> i (c o)")),
                                  waits=waits, inc=sWcb[par], inck=16)
                    wtok, free_extra = [(sWcb[par], ev_w)], []
                wcb_state["i"] += 1
                ev_mm = None
                for j in range(4):
                    h = oc * 4 + j
                    bank = pjB[(oc * 4 + j) % 4]
                    mw = list(wtok)
                    if pw(bank) is not None:
                        mw.append(pw(bank))
                    if oc == 0 and j == 0:
                        mw = mw + xw
                    for c in range(KC):
                        ev_mm = P.emit(
                            "tensor",
                            lambda e, par=par, j=j, c=c, bank=bank,
                            xnT=xnT: e.matmul(
                                bank[:],
                                wcb[par][:, c * 512 + j * 128:c * 512 + (j + 1) * 128],
                                xnT[:, c * TT:(c + 1) * TT],
                                start=(c == 0), stop=(c == KC - 1)),
                            waits=mw if c == 0 else [],
                            inc=(PE if c == KC - 1 else None))
                    if j == 3:
                        wcb_free[par] = [(PE, ev_mm)] + free_extra
                    dst = dstT[:].rearrange("d (g j) -> d g j", j=128)[
                        :, :, h * 8:h * 8 + 8]
                    src = bank[:].rearrange("d (g u) -> d g u", u=8)
                    eng = "vector" if (oc + j) % 2 == 0 else "scalar"
                    sem = VE if eng == "vector" else SE
                    if eng == "vector":
                        ev_ev = P.emit(eng, lambda e, dst=dst, src=src:
                                       e.tensor_copy(dst, src),
                                       waits=[(PE, ev_mm)], inc=sem)
                    else:
                        ev_ev = P.emit(eng, lambda e, dst=dst, src=src:
                                       e.activation(dst, src, AF.Copy),
                                       waits=[(PE, ev_mm)], inc=sem)
                    pset(bank, (sem, ev_ev))
                    evs.append((sem, ev_ev))
            if pidx < len(hooks):
                hooks[pidx]()
        qkv_done[i] = evs
        b_last[i] = (PE, ev_mm)

    def phase_C(i, hooks=()):
        evs = []
        pending = []

        def process(item):
            (b, apar, ev_nrm) = item
            tbank = ptr[b % 2]
            ev_tr = None
            for j in range(4):
                g = b * 4 + j
                tw = [(VE, ev_nrm[j]), (sCst, 32)]
                if j == 0 and pw(tbank) is not None:
                    tw.append(pw(tbank))
                P.emit("tensor", lambda e, apar=apar, j=j, tbank=tbank:
                       e.transpose(tbank[:, j * 256:j * 256 + 128],
                                   Abuf[apar][:, j * 128:(j + 1) * 128],
                                   idents[:]), waits=tw)
                ev_tr = P.emit("tensor", lambda e, g=g, j=j, tbank=tbank:
                               e.transpose(tbank[:, j * 256 + 128:(j + 1) * 256],
                                           VT[:, g * 128:(g + 1) * 128],
                                           idents[:]),
                               inc=(PE if j == 3 else None))
            ab_free[apar] = (PE, ev_tr)
            vpar = b % 2
            ew = [(PE, ev_tr)]
            if avs_free[vpar] is not None:
                ew.append(avs_free[vpar])
            eng = "vector" if b % 2 == 0 else "scalar"
            sem = VE if eng == "vector" else SE
            if eng == "vector":
                ev_ev = P.emit(eng, lambda e, vpar=vpar, tbank=tbank:
                               e.tensor_copy(AVs[vpar][:], tbank[:]),
                               waits=ew, inc=sem)
            else:
                ev_ev = P.emit(eng, lambda e, vpar=vpar, tbank=tbank:
                               e.activation(AVs[vpar][:], tbank[:], AF.Copy),
                               waits=ew, inc=sem)
            pset(tbank, (sem, ev_ev))
            pbank = pav[b % 2]
            ev_av = None
            for j in range(4):
                avw = [(sem, ev_ev)] if j == 0 else []
                if j == 0 and pw(pbank) is not None:
                    avw.append(pw(pbank))
                ev_av = P.emit("tensor", lambda e, vpar=vpar, j=j, pbank=pbank:
                               e.matmul(pbank[:, j * 128:(j + 1) * 128],
                                        AVs[vpar][:, j * 256 + 128:(j + 1) * 256],
                                        AVs[vpar][:, j * 256:j * 256 + 128],
                                        start=True, stop=True),
                               waits=avw, inc=(PE if j == 3 else None))
            avs_free[vpar] = (PE, ev_av)
            dst = attnT[:].rearrange("d (h g u) -> d g h u", h=H, u=8)[
                :, b * 4:(b + 1) * 4, :, :]
            src = pbank[:].rearrange("d (g h u) -> d g h u", h=H, u=8)
            ev_at = P.emit("scalar", lambda e, dst=dst, src=src: e.activation(
                dst, src, AF.Copy), waits=[(PE, ev_av)], inc=SE)
            pset(pbank, (SE, ev_at))
            evs.append((SE, ev_at))

        for b in range(NB):
            if len(pending) > 2:
                process(pending.pop(0))
            apar = b % 3
            sbank = ps if b % 2 == 0 else pj[0]
            sw = [] if b > 0 else list(qkv_done[i])
            if pw(sbank) is not None:
                sw.append(pw(sbank))
            ev_sc = None
            for j in range(4):
                g = b * 4 + j
                ev_sc = P.emit("tensor", lambda e, g=g, j=j, sbank=sbank:
                               e.matmul(sbank[:, j * 128:(j + 1) * 128],
                                        QT[:, g * 128:(g + 1) * 128],
                                        KT[:, g * 128:(g + 1) * 128],
                                        start=True, stop=True),
                               waits=sw if j == 0 else [],
                               inc=(PE if j == 3 else None))
            aw = [(PE, ev_sc), (VE, ev_dsc)]
            if ab_free[apar] is not None:
                aw.append(ab_free[apar])
            ev_exp = P.emit("scalar", lambda e, apar=apar, sbank=sbank:
                            e.activation(Abuf[apar][:], sbank[:], AF.Exp,
                                         scale=dsc[:, 1:2]), waits=aw, inc=SE)
            pset(sbank, (SE, ev_exp))
            ev_msk = P.emit("gpsimd", lambda e, apar=apar: e.tensor_tensor(
                Abuf[apar][:], Abuf[apar][:], masks[:], ALU.mult),
                waits=[(SE, ev_exp), (sCst, 32)], inc=PO)
            P.emit("vector", lambda e, apar=apar: e.reduce_sum(
                sums4[:], Abuf[apar][:].rearrange("p (j n) -> p j n", n=128),
                axis=AX.X), waits=[(PO, ev_msk)])
            ev_rs = P.emit("vector", lambda e: e.reciprocal(rs4[:], sums4[:]),
                           inc=VE)
            ev_nrm = []
            for j in range(4):
                evn = P.emit("vector", lambda e, apar=apar, j=j: e.tensor_scalar(
                    Abuf[apar][:, j * 128:(j + 1) * 128],
                    Abuf[apar][:, j * 128:(j + 1) * 128],
                    rs4[:, j:j + 1], None, ALU.mult),
                    waits=[(VE, ev_rs)] if j == 0 else [], inc=VE)
                ev_nrm.append(evn)
            pending.append((b, apar, ev_nrm))
            if b in (1, 5, 9):
                hi2 = (b - 1) // 4
                if hi2 < len(hooks):
                    hooks[hi2]()
        while pending:
            process(pending.pop(0))
        attn_done[i] = evs

    def phase_DE(i):
        wfid, woid = WEIGHTS.index("wf"), WEIGHTS.index("wo")
        ev_ascm = []
        base = (i % 2) * NSUB
        for m in range(NSUB):
            eva = P.emit("vector", lambda e, m=m, base=base: e.tensor_scalar(
                ascm[:, base + m:base + m + 1],
                ascale_t[:, base + m:base + m + 1], ws_bc[:, 0:1], None,
                ALU.mult), inc=VE)
            ev_ascm.append(eva)
        fbanks = [pj[0], ps]
        obanks = [pj[1], pav[0]]
        for n in range(4):
            parf = wcb_state["i"] % 3
            waits = list(w_done["wf"]) + wcb_free[parf]
            if sFT.n > 0:
                waits.append((sFT, sFT.n))
            for c in range(KC):
                P.emit("scalar", lambda e, n=n, parf=parf, c=c:
                       e.dma_start_transpose(
                           wcb[parf][:, c * 512:(c + 1) * 512],
                           wbf[3][n * 512:(n + 1) * 512,
                                  c * 128:(c + 1) * 128]),
                       waits=waits if c == 0 else [], inc=sFT, inck=16,
                       is_dma=True)
            ev_wf = sFT.n
            wcb_state["i"] += 1
            paro = wcb_state["i"] % 3
            waits = list(w_done["wo"]) + wcb_free[paro]
            ev_wo = P.emit("sync", lambda e, n=n, paro=paro: e.dma_start(
                out=wcb[paro][:],
                in_=wT[woid, n].rearrange("i c o -> i (c o)")),
                waits=waits, inc=sWcb[paro], inck=16)
            wcb_state["i"] += 1
            ev_f = ev_o = None
            ev_xrs = []
            for m in range(NSUB):
                row0 = i * TT + m * 128
                xpar = (n * NSUB + m) % 4
                ev_xrs.append(P.emit(
                    "scalar", lambda e, row0=row0, n=n, xpar=xpar:
                    e.dma_start(out=xr[xpar][:],
                                in_=x_in[row0:row0 + 128,
                                         n * 512:(n + 1) * 512]),
                    waits=[xr_free[xpar]] if xr_free[xpar] else [],
                    inc=sXr[xpar], inck=16, is_dma=True))
            for m in range(NSUB):
                row0 = i * TT + m * 128
                xpar = (n * NSUB + m) % 4
                ev_xr = ev_xrs[m]
                fbank = fbanks[m % 2]
                obank = obanks[m % 2]
                fw = [(sFT, ev_wf)]
                if n == 0 and m == 0:
                    fw.append(xq_ready[i])
                if pw(fbank) is not None:
                    fw.append(pw(fbank))
                for c in range(KC):
                    ev_f = P.emit("tensor", lambda e, parf=parf, m=m, c=c,
                                  fbank=fbank:
                                  e.matmul(fbank[:],
                                           xqT[:, c * TT + m * 128:
                                               c * TT + (m + 1) * 128],
                                           wcb[parf][:, c * 512:(c + 1) * 512],
                                           start=(c == 0), stop=(c == KC - 1)),
                                  waits=fw if c == 0 else [],
                                  inc=(PE if c == KC - 1 else None))
                ow = [(sWcb[paro], ev_wo)]
                if pw(obank) is not None:
                    ow.append(pw(obank))
                if n == 0 and m == 0:
                    ow = ow + attn_done[i]
                for c in range(KC):
                    ev_o = P.emit("tensor", lambda e, paro=paro, m=m, c=c,
                                  obank=obank:
                                  e.matmul(obank[:],
                                           attnT[:, c * TT + m * 128:
                                                 c * TT + (m + 1) * 128],
                                           wcb[paro][:, c * 512:(c + 1) * 512],
                                           start=(c == 0), stop=(c == KC - 1)),
                                  waits=ow if c == 0 else [],
                                  inc=(PE if c == KC - 1 else None))
                if m == NSUB - 1:
                    wcb_free[parf] = [(PE, ev_f)]
                    wcb_free[paro] = [(PE, ev_o)]
                    if n == 3:
                        de_ffn_last[i] = (PE, ev_f)
                cw = [(PE, ev_f), (sXr[xpar], ev_xr), (VE, ev_ascm[m])]
                ev_cb = P.emit("vector", lambda e, xpar=xpar, m=m,
                               fbank=fbank, base=base:
                               e.scalar_tensor_tensor(
                                   xr[xpar][:], fbank[:],
                                   ascm[:, base + m:base + m + 1],
                                   xr[xpar][:], ALU.mult, ALU.add),
                               waits=cw, inc=VE)
                pset(fbank, (VE, ev_cb))
                ev_fin = P.emit("vector", lambda e, xpar=xpar, obank=obank:
                                e.tensor_tensor(
                    xr[xpar][:], obank[:], xr[xpar][:], ALU.add),
                    waits=[(PE, ev_o), (VE, ev_cb)], inc=VE)
                pset(obank, (VE, ev_fin))
                stw = [(VE, ev_fin)]
                if out_prev[xpar % 2] is not None:
                    stw.append(out_prev[xpar % 2])
                ev_st = P.emit("sync", lambda e, row0=row0, n=n, xpar=xpar:
                               e.dma_start(out=out_d[row0:row0 + 128,
                                                     n * 512:(n + 1) * 512],
                                           in_=xr[xpar][:]),
                               waits=stw, inc=sOut[xpar % 2], inck=16)
                out_prev[xpar % 2] = (sOut[xpar % 2], ev_st)
                xr_free[xpar] = (sOut[xpar % 2], ev_st)

    for sub in range(NSUB):
        phase_A_slow_sub(0, sub)
    cast_done = {}
    for wi, wname in ((1, "wk"), (2, "wv"), (4, "wo")):
        cast_done[wname] = prologue_cast(wi, wname)
    for _oc in range(3):
        pe_fill_oc(_oc, _oc)
    emit_T_ins(0, 0)
    emit_T_ins(0, 1)
    for i in range(NTILE):
        nxt = i + 1

        def mk_hook_b(k, i=i, nxt=nxt):
            def h():
                if i == 0:
                    if k == 0:
                        prologue_transp(1, "wk", cast_done["wk"])
                        wf_pass1()
                        wf_pass1_fin()
                    if k == 1:
                        prologue_transp(2, "wv", cast_done["wv"])
                        wf_quant_sec(0)
                    if k == 2:
                        prologue_transp(4, "wo", cast_done["wo"])
                        wf_quant_sec(1)
                if nxt < NTILE:
                    phase_A_slow_sub(nxt, k)
            return h

        phase_B(i, [mk_hook_b(0), mk_hook_b(1), mk_hook_b(2)],
                fills={"wq": wq_fill} if i == 0 else None)
        if i == 0:
            wf_quant_sec(2)
            wfc = wf_quant_sec(3)
            w_done["wf"] = wfc
            w_done["wq"] = [(sPW, sPW.n)]
        if nxt < NTILE:
            phase_A_slow_sub(nxt, 3)
            if nxt >= 2:
                g = [b_last[nxt - 2]]
            else:
                g = [(sSw[0], sSw[0].n), (sSw[1], sSw[1].n)]
            emit_T_ins(nxt, 0, g)
        phase_C(i)
        phase_DE(i)
        if nxt < NTILE:
            emit_T_ins(nxt, 1, [de_ffn_last[i]])
    for sem in (sCst, sX, sWf32[0], sWf32[1], sWfq[0], sWfq[1],
                sWc[0], sWc[1], sTw[0], sTw[1],
                sSw[0], sSw[1], sWcb[0], sWcb[1], sWcb[2],
                sXr[0], sXr[1], sXr[2], sXr[3], sFT, sFO, sPW,
                sOut[0], sOut[1], sXo[0], sXo[1], sTi[0], sTi[1]):
        P.wait("gpsimd", sem, sem.n)
    P.wait("sync", sOut[0], sOut[0].n)
    P.wait("sync", sOut[1], sOut[1].n)

    with nc.Block() as blk:
        blk.sync(lambda e: [f(e) for f in P.streams["sync"]])
        blk.tensor(lambda e: [f(e) for f in P.streams["tensor"]])
        blk.vector(lambda e: [f(e) for f in P.streams["vector"]])
        blk.scalar(lambda e: [f(e) for f in P.streams["scalar"]])
        blk.gpsimd(lambda e: [f(e) for f in P.streams["gpsimd"]])
    st.__exit__(None, None, None)
    return nc


_NC_CACHE = None


def kernel(L0_attn_x, L0_attn_wq, L0_attn_wk, L0_attn_wv, L0_attn_wo,
           L0_attn_inv_freq, L0_ffn_weight):
    global _NC_CACHE
    assert np.asarray(L0_attn_inv_freq).shape == (64,)

    def _np(a):
        return np.ascontiguousarray(np.asarray(a).astype(np.float32, copy=False))

    x = _np(L0_attn_x).reshape(16384, D)
    ws = {"wq": _np(L0_attn_wq), "wk": _np(L0_attn_wk), "wv": _np(L0_attn_wv),
          "wo": _np(L0_attn_wo), "wf": _np(L0_ffn_weight)}
    ident = np.eye(128, dtype=np.float32).astype(ml_dtypes.bfloat16)
    m1 = (np.arange(128)[:, None] % 8 == np.arange(128)[None, :] % 8)
    maskc = np.tile(m1, (1, 4)).astype(ml_dtypes.bfloat16)

    if _NC_CACHE is None:
        _NC_CACHE = build_nc()
    nc = _NC_CACHE

    in_maps = []
    for c in range(NC):
        im = {"x": x[c * T:(c + 1) * T], "ident": ident, "maskc": maskc}
        im.update(ws)
        in_maps.append(im)
    res = run_bass_kernel_spmd(nc, in_maps, core_ids=list(range(NC)))
    out = np.concatenate([np.asarray(r["out"]) for r in res.results], axis=0)
    return out.reshape(2, 8192, 2048)


# version 25
# speedup vs baseline: 1.1396x; 1.0219x over previous
import numpy as np
import ml_dtypes
from contextlib import ExitStack

import concourse.bass as bass
import concourse.mybir as mybir
from concourse.bass_utils import run_bass_kernel_spmd

F32 = mybir.dt.float32
BF16 = mybir.dt.bfloat16
AF = mybir.ActivationFunctionType
ALU = mybir.AluOpType
AX = mybir.AxisListType

NC = 8
D = 2048
H = 16
T = 2048
TT = 512
NTILE = T // TT
NSUB = TT // 128
NG = TT // 8
NB = NG // 4
KC = D // 128
M32 = 12582912.0
M52 = 6755399441055744.0
INV_SQRT_HD = 0.08838834764831845
WEIGHTS = ("wq", "wk", "wv", "wf", "wo")


class Sem:
    def __init__(self, stack, nc, name):
        self.h = stack.enter_context(nc.semaphore(name))
        self.n = 0

    def tick(self, k=1):
        self.n += k
        return self.n


class Prog:

    def __init__(self):
        self.streams = {k: [] for k in ("sync", "tensor", "vector", "scalar",
                                        "gpsimd")}
        self.auto = {}

    def emit(self, eng, builder, waits=(), inc=None, inck=1, is_dma=False):
        auto = self.auto.get(eng)
        w = [(s, v) for (s, v) in waits if v is not None]
        incs = []
        if auto is not None and not is_dma:
            assert inc is None or inc is auto, "compute op must inc own sem"
            if auto.n > 0:
                w.append((auto, auto.n))
            incs.append((auto, 1))
            val = auto.tick(1)
        else:
            if inc is not None:
                incs.append((inc, inck))
                val = inc.tick(inck)
            else:
                val = None
        rw = tuple((s.h, v) for (s, v) in w)
        ri = tuple((s.h, k) for (s, k) in incs)

        def fn(e, builder=builder, rw=rw, ri=ri):
            for h, v in rw:
                e.wait_ge(h, v)
            inst = builder(e)
            for h, k in ri:
                inst = inst.then_inc(h, k)
        self.streams[eng].append(fn)
        return val

    def wait(self, eng, sem, val):
        self.streams[eng].append(lambda e, h=sem.h, v=val: e.wait_ge(h, v))


def build_nc():
    nc = bass.Bass("TRN2", target_bir_lowering=False)
    x_in = nc.dram_tensor("x", [T, D], F32, kind="ExternalInput").ap()
    w_in = {w: nc.dram_tensor(w, [D, D], F32, kind="ExternalInput").ap()
            for w in WEIGHTS}
    ident_in = nc.dram_tensor("ident", [128, 128], BF16, kind="ExternalInput").ap()
    mask_in = nc.dram_tensor("maskc", [128, 512], BF16, kind="ExternalInput").ap()
    out_d = nc.dram_tensor("out", [T, D], F32, kind="ExternalOutput").ap()
    wT = nc.dram_tensor("wT", [5, 4, 128, KC, 512], BF16, kind="Internal").ap()
    xstage = nc.dram_tensor("xstage", [2, 2, TT, D], BF16, kind="Internal").ap()
    wbf = nc.dram_tensor("wbf", [5, D, D], BF16, kind="Internal").ap()

    st = ExitStack()
    st.__enter__()

    def sb(name, shape, dt):
        return st.enter_context(nc.sbuf_tensor(name, shape, dt))

    def pt_(name, shape, dt):
        return st.enter_context(nc.psum_tensor(name, shape, dt))

    x_sub = sb("x_sub", [128, D], F32)
    q2 = sb("q2", [128, D], F32)
    xnTab = [sb(f"xnT{i}", [128, KC * TT], BF16) for i in range(2)]
    attnT = sb("attnT", [128, KC * TT], BF16)
    xqT = sb("xqT", [128, KC * TT], BF16)
    QT = sb("QT", [128, NG * 128], BF16)
    KT = sb("KT", [128, NG * 128], BF16)
    VT = sb("VT", [128, NG * 128], BF16)
    wcb = [sb(f"wcb{i}", [128, KC * 512], BF16) for i in range(3)]
    Abuf = [sb(f"Abuf{i}", [128, 512], BF16) for i in range(3)]
    AVs = [sb(f"AVs{i}", [128, 1024], BF16) for i in range(2)]
    idents = sb("idents", [128, 128], BF16)
    masks = sb("masks", [128, 512], BF16)
    sums4 = sb("sums4", [128, 4], F32)
    rs4 = sb("rs4", [128, 4], F32)
    small = sb("small", [128, 8], F32)
    ascale_t = sb("ascale_t", [128, 2 * NSUB], F32)
    ascm = sb("ascm", [128, 2 * NSUB], F32)
    sS = sb("sS", [128, 1], F32)
    s1mS = sb("s1mS", [128, 1], F32)
    sA = sb("sA", [128, 2], F32)
    onescol = sb("onescol", [128, 1], F32)
    onesrow = sb("onesrow", [1, 128], F32)
    wsacc = sb("wsacc", [128, 1], F32)
    wstmp = sb("wstmp", [128, 1], F32)
    ws_sc = sb("ws_sc", [1, 2], F32)
    ws_bc = sb("ws_bc", [128, 2], F32)
    dsc = sb("dsc", [128, 4], F32)
    wf32 = [sb(f"wf32{i}", [128, 1024], F32) for i in range(2)]
    wstage = [sb("wstage0", [128, 2048], BF16)]

    xr = [sb(f"xr{i}", [128, 512], F32) for i in range(4)]

    pj = [pt_(f"pj{i}", [128, 512], F32) for i in range(2)]
    ps = pt_("ps", [128, 512], F32)
    pav = [pt_(f"pav{i}", [128, 512], F32) for i in range(2)]
    ptr = [pt_(f"ptr{i}", [128, 1024], BF16) for i in range(2)]

    P = Prog()
    sCst = Sem(st, nc, "scst")
    sWf32 = [Sem(st, nc, f"swf{i}") for i in range(2)]
    sWc = [Sem(st, nc, f"swc{i}") for i in range(2)]
    sWfq = [Sem(st, nc, f"swfq{i}") for i in range(2)]
    sTw = [Sem(st, nc, f"stw{i}") for i in range(2)]
    sSw = [Sem(st, nc, f"ssw{i}") for i in range(2)]
    sWcb = [Sem(st, nc, f"swcb{i}") for i in range(3)]
    sX = Sem(st, nc, "sx")
    sXr = [Sem(st, nc, f"sxr{i}") for i in range(4)]
    sXo = [Sem(st, nc, f"sxo{i}") for i in range(2)]
    sTi = [Sem(st, nc, f"sti{i}") for i in range(2)]
    sOut = [Sem(st, nc, f"sout{i}") for i in range(2)]
    PE = Sem(st, nc, "pe")
    VE = Sem(st, nc, "ve")
    SE = Sem(st, nc, "se")
    PO = Sem(st, nc, "po")
    P.auto = {"vector": VE, "scalar": SE, "gpsimd": PO}

    cmag = sb("cmag", [128, 4], F32)
    for col, val in enumerate((M32, -M32, M52, -M52)):
        nc.const_aps.register(F32, float(val), cmag[:, col:col + 1])

    P.emit("sync", lambda e: e.dma_start(out=idents[:], in_=ident_in), inc=sCst, inck=16)
    P.emit("sync", lambda e: e.dma_start(out=masks[:], in_=mask_in), inc=sCst, inck=16)
    for col, val in enumerate((M32, -M32, M52, -M52)):
        P.emit("vector", lambda e, col=col, val=val: e.memset(cmag[:, col:col + 1], val))
    D32 = float(np.float32(np.sqrt(np.float32(128.0))))
    P.emit("vector", lambda e, D32=D32: e.memset(dsc[:, 0:1], D32))
    P.emit("vector", lambda e: e.reciprocal(dsc[:, 1:2], dsc[:, 0:1]))
    P.emit("vector", lambda e: e.memset(dsc[:, 2:3], 127.0))
    ev_dsc = P.emit("vector", lambda e: e.reciprocal(dsc[:, 3:4], dsc[:, 2:3]))
    P.emit("vector", lambda e: e.memset(sS[:], 0.7))
    P.emit("vector", lambda e: e.tensor_scalar(sS[:], sS[:], M52, -M52, ALU.add, ALU.add))
    P.emit("vector", lambda e: e.tensor_scalar(s1mS[:], sS[:], -1.0, 1.0, ALU.mult, ALU.add))
    P.emit("vector", lambda e: e.tensor_scalar(sA[:, 0:1], sS[:], M52, M32, ALU.mult, ALU.add))
    P.emit("vector", lambda e: e.tensor_scalar(sA[:, 1:2], sA[:, 0:1], -1.0, None, ALU.mult))
    P.emit("vector", lambda e: e.memset(onescol[:], 1.0))
    ev_ones = P.emit("vector", lambda e: e.memset(onesrow[:], 1.0), inc=VE)
    ev_cinit = P.emit("vector", lambda e: e.memset(wsacc[:], 0.0), inc=VE)
    P.wait("scalar", VE, ev_cinit)

    w_done = {}
    wf32_free = [None, None]
    wstage_free = [None, None]
    co_prev = [None, None]
    state = {"ev_bc_v": None, "pq_free": None}

    def prologue_cast(wi, wname):
        wsrc = w_in[wname]
        for blkr in range(4):
            cow = []
            if co_prev[blkr % 2] is not None:
                cow.append(co_prev[blkr % 2])
            ev_co = P.emit("gpsimd", lambda e, wi=wi, blkr=blkr: e.dma_start(
                out=wbf[wi][blkr * 512:(blkr + 1) * 512, :],
                in_=wsrc[blkr * 512:(blkr + 1) * 512, :]),
                waits=cow, inc=sWc[blkr % 2], inck=16, is_dma=True)
            co_prev[blkr % 2] = (sWc[blkr % 2], ev_co)
        return [(sWc[0], sWc[0].n), (sWc[1], sWc[1].n)]

    def wf_pass1(wf_cast_done):
        ev_acc = None
        lds = []
        for idx in range(32):
            r2, hh, par = idx // 2, idx % 2, idx % 2
            w = list(wf_cast_done) if idx < 2 else []
            if len(lds) >= 2:
                w.append(lds[-2])
            ev_ld = P.emit("gpsimd", lambda e, r2=r2, hh=hh, par=par: e.dma_start(
                out=AVs[par][:],
                in_=wbf[3][r2 * 128:(r2 + 1) * 128, hh * 1024:(hh + 1) * 1024]),
                waits=w, inc=sWf32[par], inck=16, is_dma=True)
            ev_red = P.emit("vector", lambda e, par=par: e.reduce_sum(
                wstmp[:], AVs[par][:], axis=AX.X,
                apply_absolute_value=True), waits=[(sWf32[par], ev_ld)], inc=VE)
            lds.append((VE, ev_red))
            ev_acc = P.emit("vector", lambda e: e.tensor_tensor(
                wsacc[:], wsacc[:], wstmp[:], ALU.add), inc=VE)
            acc_evt = (VE, ev_acc)
        state["pass1_loads"] = [(sWf32[0], sWf32[0].n), (sWf32[1], sWf32[1].n)]
        avs_free[0] = lds[-2]
        avs_free[1] = lds[-1]
        ev_ws_mm = P.emit("tensor", lambda e: e.matmul(
            pav[0][0:1, 0:1], wsacc[:], onescol[:],
            start=True, stop=True), waits=[acc_evt, (VE, ev_ones)], inc=PE)
        P.emit("vector", lambda e: e.tensor_scalar(
            ws_sc[0:1, 0:1], pav[0][0:1, 0:1], 1.0 / (D * D), 1e-6,
            ALU.mult, ALU.max), waits=[(PE, ev_ws_mm)])
        P.emit("vector", lambda e: e.reciprocal(ws_sc[0:1, 1:2], ws_sc[0:1, 0:1]))
        ev_sc2 = P.emit("vector", lambda e: e.tensor_scalar(
            ws_sc[0:1, 0:1], ws_sc[0:1, 0:1], dsc[0:1, 3:4], None, ALU.mult),
            inc=VE)
        ev_bc = P.emit("tensor", lambda e: e.matmul(
            pav[0][:, 0:2], onesrow[:], ws_sc[0:1, :], start=True, stop=True),
            waits=[(VE, ev_sc2)], inc=PE)
        ev_bc_v = P.emit("vector", lambda e: e.tensor_copy(
            ws_bc[:], pav[0][:, 0:2]), waits=[(PE, ev_bc)], inc=VE)
        state["ev_bc_v"] = (VE, ev_bc_v)
        pfree["pav0"] = (VE, ev_bc_v)

    def wf_quant_sec(k):
        for idx in range(8 * k, 8 * k + 8):
            r, hf, par = idx // 2, idx % 2, idx % 2
            ev_ld = P.emit("gpsimd", lambda e, r=r, hf=hf, par=par: e.dma_start(
                out=wf32[par][:],
                in_=w_in["wf"][r * 128:(r + 1) * 128, hf * 1024:(hf + 1) * 1024]),
                waits=[wf32_free[par]] if wf32_free[par] else [],
                inc=sWfq[par], inck=16, is_dma=True)
            qw = [(sWfq[par], ev_ld), state["ev_bc_v"]]
            if state["pq_free"] is not None:
                qw.append(state["pq_free"])
            if idx == 8 * k and x_state["q2_out"] is not None:
                qw.append(x_state["q2_out"])
            P.emit("vector", lambda e, par=par: e.tensor_scalar(
                q2[:, 0:1024], wf32[par][:], ws_bc[:, 1:2], 0.5, ALU.mult, ALU.is_ge),
                waits=qw)
            ev_q = P.emit("vector", lambda e, par=par: e.tensor_scalar(
                q2[:, 1024:2048], wf32[par][:], ws_bc[:, 1:2], -0.5, ALU.mult, ALU.is_le),
                inc=VE)
            wf32_free[par] = (VE, ev_q)
            ev_cl = P.emit("vector", lambda e: e.tensor_tensor(
                q2[:, 1024:2048], q2[:, 0:1024], q2[:, 1024:2048], ALU.subtract), inc=VE)
            cow = [(VE, ev_cl)]
            if idx < 2 and "pass1_loads" in state:
                cow += state["pass1_loads"]
            if co_prev[par] is not None:
                cow.append(co_prev[par])
            ev_co = P.emit("gpsimd", lambda e, r=r, hf=hf:
                           e.dma_start(out=wbf[3][r * 128:(r + 1) * 128,
                                                  hf * 1024:(hf + 1) * 1024],
                                       in_=q2[:, 1024:2048]),
                           waits=cow, inc=sWc[par], inck=16, is_dma=True)
            co_prev[par] = (sWc[par], ev_co)
            state["pq_free"] = (sWc[par], ev_co)
        x_state["q2_extra"] = state["pq_free"]
        return [(sWc[0], sWc[0].n), (sWc[1], sWc[1].n)]

    def prologue_transp(wi, wname, cast_done):
        for c in range(KC):
            w = list(cast_done) if c == 0 else []
            if wstage_free[0] is not None:
                w.append(wstage_free[0])
            ev_ti = P.emit("sync", lambda e, wi=wi, c=c:
                           e.dma_start_transpose(
                               wstage[0][:], wbf[wi][:, c * 128:(c + 1) * 128]),
                           waits=w, inc=sTw[0], inck=16)
            dst = wT[wi][:, :, c, :].rearrange("oc i o -> i oc o")
            ev_st = P.emit("sync", lambda e, dst=dst: e.dma_start(
                out=dst, in_=wstage[0][:].rearrange("i (oc o) -> i oc o", o=512)),
                waits=[(sTw[0], ev_ti)], inc=sSw[0], inck=16)
            wstage_free[0] = (sSw[0], ev_st)
        w_done[wname] = [(sSw[0], sSw[0].n)]
    pfree = {}

    def pw(bank):
        return pfree.get(bank.name)

    def pset(bank, evt):
        pfree[bank.name] = evt

    wcb_free = [None, None, None]
    ab_free = [None, None, None]
    avs_free = [None, None]
    xr_free = [None, None, None, None]
    out_prev = [None, None]
    wcb_state = {"i": 0}
    x_state = {"xn_last_s": None, "xn_out": None, "q2_out": None}
    xn_ready = {}
    xq_ready = {}
    slot_free = {}
    slot_parts = {}
    qkv_done = {}
    attn_done = {}
    b_last = {}
    de_ffn_last = {}

    a_ready = {}

    def phase_A_slow_sub(i, s):
        row0 = i * TT + s * 128
        sc = (i % 2) * NSUB + s
        xw = [e for e in (x_state["xn_last_s"], x_state["xn_out"]) if e]
        ev_x = P.emit("scalar", lambda e, row0=row0: e.dma_start(
            out=x_sub[:], in_=x_in[row0:row0 + 128, :]), waits=xw,
            inc=sX, inck=16, is_dma=True)
        sqw = [(sX, ev_x)]
        if x_state["q2_out"] is not None:
            sqw.append(x_state["q2_out"])
        if x_state.get("q2_extra") is not None:
            sqw.append(x_state.pop("q2_extra"))
        ev_sq = P.emit("scalar", lambda e: e.activation(
            q2[:], x_sub[:], AF.Square), waits=sqw, inc=SE)
        P.emit("vector", lambda e: e.reduce_sum(
            small[:, 0:1], x_sub[:], axis=AX.X), waits=[(sX, ev_x)])
        P.emit("vector", lambda e: e.reduce_sum(
            small[:, 1:2], q2[:], axis=AX.X), waits=[(SE, ev_sq)])
        P.emit("vector", lambda e: e.tensor_scalar(
            small[:, 0:1], small[:, 0:1], 1.0 / D, None, ALU.mult))
        P.emit("vector", lambda e: e.tensor_scalar(
            small[:, 2:3], small[:, 0:1], small[:, 0:1], -1e-5,
            ALU.mult, ALU.add))
        ev_var = P.emit("vector", lambda e: e.tensor_scalar(
            small[:, 1:2], small[:, 1:2], 1.0 / D, small[:, 2:3],
            ALU.mult, ALU.subtract), inc=VE)
        P.emit("scalar", lambda e: e.activation(
            small[:, 3:4], small[:, 1:2], AF.Ln), waits=[(VE, ev_var)])
        ev_rstd = P.emit("scalar", lambda e: e.activation(
            small[:, 3:4], small[:, 3:4], AF.Exp, scale=-0.5), inc=SE)
        ev_ngm = P.emit("vector", lambda e: e.tensor_scalar(
            small[:, 2:3], small[:, 0:1], small[:, 3:4], -1.0,
            ALU.mult, ALU.mult), waits=[(SE, ev_rstd)], inc=VE)
        ev_xn = P.emit("vector", lambda e: e.tensor_scalar(
            x_sub[:], x_sub[:], small[:, 3:4], small[:, 2:3],
            ALU.mult, ALU.add), waits=[(VE, ev_ngm)], inc=VE)
        P.emit("vector", lambda e: e.reduce_max(
            small[:, 4:5], x_sub[:], axis=AX.X, apply_absolute_value=True),
            waits=[(VE, ev_xn)])
        P.emit("vector", lambda e, sc=sc: e.tensor_scalar(
            ascale_t[:, sc:sc + 1], small[:, 4:5], 1e-8, None, ALU.max))
        P.emit("vector", lambda e, sc=sc: e.reciprocal(
            small[:, 5:6], ascale_t[:, sc:sc + 1]))
        P.emit("vector", lambda e: e.tensor_scalar(
            small[:, 5:6], small[:, 5:6], 127.0, None, ALU.mult))
        ev_z = P.emit("vector", lambda e: e.tensor_scalar(
            q2[:], x_sub[:], small[:, 5:6], None, ALU.mult), inc=VE)
        x_state["xn_last_s"] = (VE, ev_z)
        P.emit("vector", lambda e: e.tensor_scalar(
            q2[:], q2[:], sA[:, 0:1], sA[:, 1:2], ALU.add, ALU.add))
        ev_xq = P.emit("vector", lambda e: e.tensor_scalar(
            q2[:], q2[:], 127.0, -127.0, ALU.min, ALU.max), inc=VE)
        par = i % 2
        ow = [(VE, ev_xn)] + slot_free.get(par, [])
        ev_oxn = P.emit("gpsimd", lambda e, par=par, s=s: e.dma_start(
            out=xstage[par, 0, s * 128:(s + 1) * 128, :], in_=x_sub[:]),
            waits=ow, inc=sXo[0], inck=16, is_dma=True)
        x_state["xn_out"] = (sXo[0], ev_oxn)
        ev_oxq = P.emit("gpsimd", lambda e, par=par, s=s: e.dma_start(
            out=xstage[par, 1, s * 128:(s + 1) * 128, :], in_=q2[:]),
            waits=[(VE, ev_xq)], inc=sXo[1], inck=16, is_dma=True)
        x_state["q2_out"] = (sXo[1], ev_oxq)
        if s == 3:
            xn_ready[i] = (sXo[0], sXo[0].n)
            xq_ready[i] = (sXo[1], sXo[1].n)

    def emit_T_ins(i, which, gate):
        par = i % 2
        dstT = xnTab[i % 2] if which == 0 else xqT
        sem = sTi[which]
        for c in range(KC):
            w = [xn_ready[i] if which == 0 else xq_ready[i]]
            if gate is not None:
                w.append(gate)
            P.emit("sync", lambda e, c=c, par=par, dstT=dstT, which=which:
                   e.dma_start_transpose(
                dstT[:, c * TT:(c + 1) * TT],
                xstage[par, which][:, c * 128:(c + 1) * 128]),
                waits=w if c == 0 else [], inc=sem, inck=16)
        ready = (sem, sem.n)
        if which == 0:
            xn_ready[i] = ready
        else:
            xq_ready[i] = ready
        slot_parts[(par, which)] = ready
        slot_free[par] = [v for v in (slot_parts.get((par, 0)),
                                      slot_parts.get((par, 1))) if v]

    def phase_B(i, hooks=()):
        evs = []
        xnT = xnTab[i % 2]
        xw = [xn_ready[i]]
        pjB = [pj[0], pj[1], ps, pav[0]]
        for pidx, (wname, dstT) in enumerate((("wq", QT), ("wk", KT),
                                              ("wv", VT))):
            wi = WEIGHTS.index(wname)
            for oc in range(4):
                par = wcb_state["i"] % 3
                waits = list(w_done[wname])
                if wcb_free[par] is not None:
                    waits.append(wcb_free[par])
                ev_w = P.emit("sync", lambda e, wi=wi, oc=oc, par=par:
                              e.dma_start(out=wcb[par][:],
                                          in_=wT[wi, oc].rearrange(
                                              "i c o -> i (c o)")),
                              waits=waits, inc=sWcb[par], inck=16)
                wcb_state["i"] += 1
                ev_mm = None
                for j in range(4):
                    h = oc * 4 + j
                    bank = pjB[(oc * 4 + j) % 4]
                    mw = [(sWcb[par], ev_w)]
                    if pw(bank) is not None:
                        mw.append(pw(bank))
                    if oc == 0 and j == 0:
                        mw = mw + xw
                    for c in range(KC):
                        ev_mm = P.emit(
                            "tensor",
                            lambda e, par=par, j=j, c=c, bank=bank,
                            xnT=xnT: e.matmul(
                                bank[:],
                                wcb[par][:, c * 512 + j * 128:c * 512 + (j + 1) * 128],
                                xnT[:, c * TT:(c + 1) * TT],
                                start=(c == 0), stop=(c == KC - 1)),
                            waits=mw if c == 0 else [],
                            inc=(PE if c == KC - 1 else None))
                    if j == 3:
                        wcb_free[par] = (PE, ev_mm)
                    dst = dstT[:].rearrange("d (g j) -> d g j", j=128)[
                        :, :, h * 8:h * 8 + 8]
                    src = bank[:].rearrange("d (g u) -> d g u", u=8)
                    eng = "vector" if (oc + j) % 2 == 0 else "scalar"
                    sem = VE if eng == "vector" else SE
                    if eng == "vector":
                        ev_ev = P.emit(eng, lambda e, dst=dst, src=src:
                                       e.tensor_copy(dst, src),
                                       waits=[(PE, ev_mm)], inc=sem)
                    else:
                        ev_ev = P.emit(eng, lambda e, dst=dst, src=src:
                                       e.activation(dst, src, AF.Copy),
                                       waits=[(PE, ev_mm)], inc=sem)
                    pset(bank, (sem, ev_ev))
                    evs.append((sem, ev_ev))
            if pidx < len(hooks):
                hooks[pidx]()
        qkv_done[i] = evs
        b_last[i] = (PE, ev_mm)

    def phase_C(i, hooks=()):
        evs = []
        pending = []

        def process(item):
            (b, apar, ev_nrm) = item
            tbank = ptr[b % 2]
            ev_tr = None
            for j in range(4):
                g = b * 4 + j
                tw = [(VE, ev_nrm[j]), (sCst, 32)]
                if j == 0 and pw(tbank) is not None:
                    tw.append(pw(tbank))
                P.emit("tensor", lambda e, apar=apar, j=j, tbank=tbank:
                       e.transpose(tbank[:, j * 256:j * 256 + 128],
                                   Abuf[apar][:, j * 128:(j + 1) * 128],
                                   idents[:]), waits=tw)
                ev_tr = P.emit("tensor", lambda e, g=g, j=j, tbank=tbank:
                               e.transpose(tbank[:, j * 256 + 128:(j + 1) * 256],
                                           VT[:, g * 128:(g + 1) * 128],
                                           idents[:]),
                               inc=(PE if j == 3 else None))
            ab_free[apar] = (PE, ev_tr)
            vpar = b % 2
            ew = [(PE, ev_tr)]
            if avs_free[vpar] is not None:
                ew.append(avs_free[vpar])
            eng = "vector" if b % 2 == 0 else "scalar"
            sem = VE if eng == "vector" else SE
            if eng == "vector":
                ev_ev = P.emit(eng, lambda e, vpar=vpar, tbank=tbank:
                               e.tensor_copy(AVs[vpar][:], tbank[:]),
                               waits=ew, inc=sem)
            else:
                ev_ev = P.emit(eng, lambda e, vpar=vpar, tbank=tbank:
                               e.activation(AVs[vpar][:], tbank[:], AF.Copy),
                               waits=ew, inc=sem)
            pset(tbank, (sem, ev_ev))
            pbank = pav[b % 2]
            ev_av = None
            for j in range(4):
                avw = [(sem, ev_ev)] if j == 0 else []
                if j == 0 and pw(pbank) is not None:
                    avw.append(pw(pbank))
                ev_av = P.emit("tensor", lambda e, vpar=vpar, j=j, pbank=pbank:
                               e.matmul(pbank[:, j * 128:(j + 1) * 128],
                                        AVs[vpar][:, j * 256 + 128:(j + 1) * 256],
                                        AVs[vpar][:, j * 256:j * 256 + 128],
                                        start=True, stop=True),
                               waits=avw, inc=(PE if j == 3 else None))
            avs_free[vpar] = (PE, ev_av)
            dst = attnT[:].rearrange("d (h g u) -> d g h u", h=H, u=8)[
                :, b * 4:(b + 1) * 4, :, :]
            src = pbank[:].rearrange("d (g h u) -> d g h u", h=H, u=8)
            ev_at = P.emit("scalar", lambda e, dst=dst, src=src: e.activation(
                dst, src, AF.Copy), waits=[(PE, ev_av)], inc=SE)
            pset(pbank, (SE, ev_at))
            evs.append((SE, ev_at))

        for b in range(NB):
            if len(pending) > 2:
                process(pending.pop(0))
            apar = b % 3
            sbank = ps if b % 2 == 0 else pj[0]
            sw = [] if b > 0 else list(qkv_done[i])
            if pw(sbank) is not None:
                sw.append(pw(sbank))
            ev_sc = None
            for j in range(4):
                g = b * 4 + j
                ev_sc = P.emit("tensor", lambda e, g=g, j=j, sbank=sbank:
                               e.matmul(sbank[:, j * 128:(j + 1) * 128],
                                        QT[:, g * 128:(g + 1) * 128],
                                        KT[:, g * 128:(g + 1) * 128],
                                        start=True, stop=True),
                               waits=sw if j == 0 else [],
                               inc=(PE if j == 3 else None))
            aw = [(PE, ev_sc), (VE, ev_dsc)]
            if ab_free[apar] is not None:
                aw.append(ab_free[apar])
            ev_exp = P.emit("scalar", lambda e, apar=apar, sbank=sbank:
                            e.activation(Abuf[apar][:], sbank[:], AF.Exp,
                                         scale=dsc[:, 1:2]), waits=aw, inc=SE)
            pset(sbank, (SE, ev_exp))
            ev_msk = P.emit("gpsimd", lambda e, apar=apar: e.tensor_tensor(
                Abuf[apar][:], Abuf[apar][:], masks[:], ALU.mult),
                waits=[(SE, ev_exp), (sCst, 32)], inc=PO)
            P.emit("vector", lambda e, apar=apar: e.reduce_sum(
                sums4[:], Abuf[apar][:].rearrange("p (j n) -> p j n", n=128),
                axis=AX.X), waits=[(PO, ev_msk)])
            ev_rs = P.emit("vector", lambda e: e.reciprocal(rs4[:], sums4[:]),
                           inc=VE)
            ev_nrm = []
            for j in range(4):
                evn = P.emit("vector", lambda e, apar=apar, j=j: e.tensor_scalar(
                    Abuf[apar][:, j * 128:(j + 1) * 128],
                    Abuf[apar][:, j * 128:(j + 1) * 128],
                    rs4[:, j:j + 1], None, ALU.mult),
                    waits=[(VE, ev_rs)] if j == 0 else [], inc=VE)
                ev_nrm.append(evn)
            pending.append((b, apar, ev_nrm))
            if b in (1, 5, 9):
                hi2 = (b - 1) // 4
                if hi2 < len(hooks):
                    hooks[hi2]()
        while pending:
            process(pending.pop(0))
        attn_done[i] = evs

    def phase_DE(i):
        wfid, woid = WEIGHTS.index("wf"), WEIGHTS.index("wo")
        ev_ascm = []
        base = (i % 2) * NSUB
        for m in range(NSUB):
            eva = P.emit("vector", lambda e, m=m, base=base: e.tensor_scalar(
                ascm[:, base + m:base + m + 1],
                ascale_t[:, base + m:base + m + 1], ws_bc[:, 0:1], None,
                ALU.mult), inc=VE)
            ev_ascm.append(eva)
        fbanks = [pj[0], ps]
        obanks = [pj[1], pav[0]]
        for n in range(4):
            parf = wcb_state["i"] % 3
            waits = list(w_done["wf"])
            if wcb_free[parf] is not None:
                waits.append(wcb_free[parf])
            ev_wf = P.emit("sync", lambda e, n=n, parf=parf: e.dma_start(
                out=wcb[parf][:], in_=wT[wfid, n].rearrange("i c o -> i (c o)")),
                waits=waits, inc=sWcb[parf], inck=16)
            wcb_state["i"] += 1
            paro = wcb_state["i"] % 3
            waits = list(w_done["wo"])
            if wcb_free[paro] is not None:
                waits.append(wcb_free[paro])
            ev_wo = P.emit("sync", lambda e, n=n, paro=paro: e.dma_start(
                out=wcb[paro][:], in_=wT[woid, n].rearrange("i c o -> i (c o)")),
                waits=waits, inc=sWcb[paro], inck=16)
            wcb_state["i"] += 1
            ev_f = ev_o = None
            ev_xrs = []
            for m in range(NSUB):
                row0 = i * TT + m * 128
                xpar = (n * NSUB + m) % 4
                ev_xrs.append(P.emit(
                    "scalar", lambda e, row0=row0, n=n, xpar=xpar:
                    e.dma_start(out=xr[xpar][:],
                                in_=x_in[row0:row0 + 128,
                                         n * 512:(n + 1) * 512]),
                    waits=[xr_free[xpar]] if xr_free[xpar] else [],
                    inc=sXr[xpar], inck=16, is_dma=True))
            for m in range(NSUB):
                row0 = i * TT + m * 128
                xpar = (n * NSUB + m) % 4
                ev_xr = ev_xrs[m]
                fbank = fbanks[m % 2]
                obank = obanks[m % 2]
                fw = [(sWcb[parf], ev_wf)]
                if n == 0 and m == 0:
                    fw.append(xq_ready[i])
                if pw(fbank) is not None:
                    fw.append(pw(fbank))
                for c in range(KC):
                    ev_f = P.emit("tensor", lambda e, parf=parf, m=m, c=c,
                                  fbank=fbank:
                                  e.matmul(fbank[:],
                                           xqT[:, c * TT + m * 128:
                                               c * TT + (m + 1) * 128],
                                           wcb[parf][:, c * 512:(c + 1) * 512],
                                           start=(c == 0), stop=(c == KC - 1)),
                                  waits=fw if c == 0 else [],
                                  inc=(PE if c == KC - 1 else None))
                ow = [(sWcb[paro], ev_wo)]
                if pw(obank) is not None:
                    ow.append(pw(obank))
                if n == 0 and m == 0:
                    ow = ow + attn_done[i]
                for c in range(KC):
                    ev_o = P.emit("tensor", lambda e, paro=paro, m=m, c=c,
                                  obank=obank:
                                  e.matmul(obank[:],
                                           attnT[:, c * TT + m * 128:
                                                 c * TT + (m + 1) * 128],
                                           wcb[paro][:, c * 512:(c + 1) * 512],
                                           start=(c == 0), stop=(c == KC - 1)),
                                  waits=ow if c == 0 else [],
                                  inc=(PE if c == KC - 1 else None))
                if m == NSUB - 1:
                    wcb_free[parf] = (PE, ev_f)
                    wcb_free[paro] = (PE, ev_o)
                    if n == 3:
                        de_ffn_last[i] = (PE, ev_f)
                cw = [(PE, ev_f), (sXr[xpar], ev_xr), (VE, ev_ascm[m])]
                ev_cb = P.emit("vector", lambda e, xpar=xpar, m=m,
                               fbank=fbank, base=base:
                               e.scalar_tensor_tensor(
                                   xr[xpar][:], fbank[:],
                                   ascm[:, base + m:base + m + 1],
                                   xr[xpar][:], ALU.mult, ALU.add),
                               waits=cw, inc=VE)
                pset(fbank, (VE, ev_cb))
                ev_fin = P.emit("vector", lambda e, xpar=xpar, obank=obank:
                                e.tensor_tensor(
                    xr[xpar][:], obank[:], xr[xpar][:], ALU.add),
                    waits=[(PE, ev_o), (VE, ev_cb)], inc=VE)
                pset(obank, (VE, ev_fin))
                stw = [(VE, ev_fin)]
                if out_prev[xpar % 2] is not None:
                    stw.append(out_prev[xpar % 2])
                ev_st = P.emit("sync", lambda e, row0=row0, n=n, xpar=xpar:
                               e.dma_start(out=out_d[row0:row0 + 128,
                                                     n * 512:(n + 1) * 512],
                                           in_=xr[xpar][:]),
                               waits=stw, inc=sOut[xpar % 2], inck=16)
                out_prev[xpar % 2] = (sOut[xpar % 2], ev_st)
                xr_free[xpar] = (sOut[xpar % 2], ev_st)

    for sub in range(NSUB):
        phase_A_slow_sub(0, sub)
    cast_done = {}
    for wi, wname in ((0, "wq"), (3, "wf"), (1, "wk"), (2, "wv"), (4, "wo")):
        cast_done[wname] = prologue_cast(wi, wname)
    wf_pass1(cast_done["wf"])
    prologue_transp(0, "wq", cast_done["wq"])
    emit_T_ins(0, 0, None)
    emit_T_ins(0, 1, None)
    for i in range(NTILE):
        nxt = i + 1

        def mk_hook_b(k, i=i, nxt=nxt):
            def h():
                if i == 0:
                    wf_quant_sec(k)
                    prologue_transp((1, 2, 4)[k], ("wk", "wv", "wo")[k],
                                    cast_done[("wk", "wv", "wo")[k]])
                if nxt < NTILE:
                    phase_A_slow_sub(nxt, k)
            return h

        phase_B(i, [mk_hook_b(0), mk_hook_b(1), mk_hook_b(2)])
        if i == 0:
            wfc = wf_quant_sec(3)
            prologue_transp(3, "wf", wfc)
        if nxt < NTILE:
            phase_A_slow_sub(nxt, 3)
            emit_T_ins(nxt, 0, b_last[nxt - 2] if nxt >= 2 else None)
        phase_C(i)
        phase_DE(i)
        if nxt < NTILE:
            emit_T_ins(nxt, 1, de_ffn_last[i])
    for sem in (sCst, sX, sWf32[0], sWf32[1], sWfq[0], sWfq[1],
                sWc[0], sWc[1], sTw[0], sTw[1],
                sSw[0], sSw[1], sWcb[0], sWcb[1], sWcb[2],
                sXr[0], sXr[1], sXr[2], sXr[3],
                sOut[0], sOut[1], sXo[0], sXo[1], sTi[0], sTi[1]):
        P.wait("gpsimd", sem, sem.n)
    P.wait("sync", sOut[0], sOut[0].n)
    P.wait("sync", sOut[1], sOut[1].n)

    with nc.Block() as blk:
        blk.sync(lambda e: [f(e) for f in P.streams["sync"]])
        blk.tensor(lambda e: [f(e) for f in P.streams["tensor"]])
        blk.vector(lambda e: [f(e) for f in P.streams["vector"]])
        blk.scalar(lambda e: [f(e) for f in P.streams["scalar"]])
        blk.gpsimd(lambda e: [f(e) for f in P.streams["gpsimd"]])
    st.__exit__(None, None, None)
    return nc


_NC_CACHE = None


def kernel(L0_attn_x, L0_attn_wq, L0_attn_wk, L0_attn_wv, L0_attn_wo,
           L0_attn_inv_freq, L0_ffn_weight):
    global _NC_CACHE
    assert np.asarray(L0_attn_inv_freq).shape == (64,)

    def _np(a):
        return np.ascontiguousarray(np.asarray(a).astype(np.float32, copy=False))

    x = _np(L0_attn_x).reshape(16384, D)
    ws = {"wq": _np(L0_attn_wq), "wk": _np(L0_attn_wk), "wv": _np(L0_attn_wv),
          "wo": _np(L0_attn_wo), "wf": _np(L0_ffn_weight)}
    ident = np.eye(128, dtype=np.float32).astype(ml_dtypes.bfloat16)
    m1 = (np.arange(128)[:, None] % 8 == np.arange(128)[None, :] % 8)
    maskc = np.tile(m1, (1, 4)).astype(ml_dtypes.bfloat16)

    if _NC_CACHE is None:
        _NC_CACHE = build_nc()
    nc = _NC_CACHE

    in_maps = []
    for c in range(NC):
        im = {"x": x[c * T:(c + 1) * T], "ident": ident, "maskc": maskc}
        im.update(ws)
        in_maps.append(im)
    res = run_bass_kernel_spmd(nc, in_maps, core_ids=list(range(NC)))
    out = np.concatenate([np.asarray(r["out"]) for r in res.results], axis=0)
    return out.reshape(2, 8192, 2048)
